# version 28
# baseline (speedup 1.0000x reference)
from contextlib import ExitStack
import numpy as np

import concourse.bass as bass
import concourse.mybir as mybir
from concourse.bass_utils import run_bass_kernel_spmd

F32 = mybir.dt.float32
BF16 = mybir.dt.bfloat16
FP8 = mybir.dt.float8e4
AF = mybir.ActivationFunctionType
ALU = mybir.AluOpType
AX = mybir.AxisListType.X

S = 2048
D = 1024
H = 16
HD = 64
BPC = 2
NT = 16
NI = 4

SUMX, SUMSQ, NEGMEAN, SS_, M2, VAR, LNV, RSTD, NEGMRS = range(9)
C_EPS, C_M05, C_ESC, C_EBI, C_M1 = range(5)

ENGINES = ("sync", "gpsimd", "scalar", "vector", "tensor")
import os as _os
DEFER_FFN = _os.environ.get("DEFER_FFN", "1") == "1"


class Em:

    def __init__(self, nc, es):
        self.nc = nc
        self.q = {e: [] for e in ENGINES}
        self.cnt = {}
        self.sems = {}
        self.lastwait = {e: {} for e in ENGINES}
        for s_ in ("sx", "sxr", "so", "sw", "sp", "st", "ss", "sv", "sg"):
            h = es.enter_context(nc.semaphore(s_))
            self.sems[s_] = h
            self.cnt[s_] = 0

    def c(self, name):
        return self.cnt[name]

    def op(self, eng, fn, waits=(), inc=None, n=1):
        wl = [(s, v) for (s, v) in waits if v > 0]
        self.q[eng].append((wl, fn, inc, n))
        if inc is not None:
            self.cnt[inc] += n
        return self.cnt[inc] if inc is not None else None

    def wait(self, eng, *waits):
        self.op(eng, None, waits=waits)

    def flush(self):
        if not any(self.q[e] for e in ENGINES):
            return
        with self.nc.Block() as blk:
            for eng in ENGINES:
                ops = self.q[eng]
                if not ops:
                    continue
                sec = getattr(blk, eng)

                def body(e, ops=ops, eng=eng):
                    lw = self.lastwait[eng]
                    for (wl, fn, inc, n) in ops:
                        for (sname, v) in wl:
                            if v > lw.get(sname, -1):
                                e.wait_ge(self.sems[sname], v)
                                lw[sname] = v
                        if fn is not None:
                            instr = fn(e)
                            if inc is not None:
                                instr.then_inc(self.sems[inc], n)

                sec(body)
                self.q[eng] = []


def build_nc(debug=False):
    nc = bass.Bass("TRN2", target_bir_lowering=False, detect_race_conditions=False)

    x = nc.dram_tensor("x", [BPC, S, D], F32, kind="ExternalInput")
    wq = nc.dram_tensor("wq", [D, D], F32, kind="ExternalInput")
    wk = nc.dram_tensor("wk", [D, D], F32, kind="ExternalInput")
    wv = nc.dram_tensor("wv", [D, D], F32, kind="ExternalInput")
    wo = nc.dram_tensor("wo", [D, D], F32, kind="ExternalInput")
    wff = nc.dram_tensor("wff", [D, D], F32, kind="ExternalInput")
    bias = nc.dram_tensor("bias", [D], F32, kind="ExternalInput")
    out = nc.dram_tensor("out", [BPC, S, D], F32, kind="ExternalOutput")
    wdTs = nc.dram_tensor("wdTs", [128, 8 * D], BF16, kind="Internal")
    dbg = {}
    if debug:
        for nm, shp, dt in (("d_h1f", [128, 8 * S], BF16), ("d_qT", [128, 8 * S], BF16),
                            ("d_kT", [128, 8 * S], BF16), ("d_vaug", [128, NT * 1040], BF16),
                            ("d_onorm", [128, 8 * 512], BF16), ("d_r1", [128, 4 * D], F32),
                            ("d_xd", [128, D], BF16), ("d_wdT", [128, 8 * D], BF16),
                            ("d_e", [128, 4 * 1024], BF16),
                            ("d_ps", [128, 1024], F32), ("d_xdT", [128, D], BF16)):
            dbg[nm] = nc.dram_tensor(nm, shp, dt, kind="ExternalOutput")

    def x_tile_ap(lb, row0):
        return bass.AP(x, lb * S * D + row0 * D, [[D, 128], [1, D]])

    def x_block_ap(lb, t0, ntiles):
        return bass.AP(x, lb * S * D + t0 * 128 * D,
                       [[D, 128], [128 * D, ntiles], [1, D]])

    def out_tile_ap(lb, row0):
        return bass.AP(out, lb * S * D + row0 * D, [[D, 128], [1, D]])

    es = ExitStack()
    with es:
        E = Em(nc, es)
        sb = lambda name, shape, dt: es.enter_context(nc.sbuf_tensor(name, shape, dt))
        pst = lambda name, shape, dt: es.enter_context(nc.psum_tensor(name, shape, dt))

        qT = sb("qT", [128, 8 * S], BF16)
        kT = sb("kT", [128, 8 * S], BF16)
        v_aug = sb("v_aug", [128, NT * 1040], BF16)
        biasT = sb("biasT", [1, D], BF16)
        cst = sb("cst", [128, 8], F32)
        rowi = sb("rowi", [128, 128], BF16)
        coli = sb("coli", [128, 128], BF16)
        identb = sb("identb", [128, 128], BF16)
        trib = sb("trib", [128, 128], BF16)
        ones1 = sb("ones1", [1, 128], BF16)
        indA = sb("indA", [1, 128], BF16)
        indB = sb("indB", [1, 128], BF16)
        rbAB = sb("rbAB", [1, 1024], BF16)
        statsP = [sb(f"stats{p}", [128, 12], F32) for p in range(4)]

        psS = pst("psS", [128, 2048], F32)
        accAB = pst("accAB", [128, 1024], F32)
        pAux = pst("pAux", [128, 1024], F32)
        ptxv = pAux[:, :].bitcast(BF16)

        pstx_free = [("sv", 0), ("sv", 0)]

        E.op("gpsimd", lambda e: e.iota(rowi[:, :], [[0, 128]], channel_multiplier=1,
                                        allow_small_or_imprecise_dtypes=True))
        E.op("gpsimd", lambda e: e.iota(coli[:, :], [[1, 128]], channel_multiplier=0,
                                        allow_small_or_imprecise_dtypes=True))
        E.op("gpsimd", lambda e: e.memset(cst[:, C_EPS:C_EPS + 1], 1e-5))
        E.op("gpsimd", lambda e: e.memset(cst[:, C_M05:C_M05 + 1], -0.5))
        E.op("gpsimd", lambda e: e.memset(cst[:, C_ESC:C_ESC + 1], 0.125))
        E.op("gpsimd", lambda e: e.memset(cst[:, C_EBI:C_EBI + 1], -12.0))
        E.op("gpsimd", lambda e: e.memset(cst[:, C_M1:C_M1 + 1], -1.0))
        E.op("gpsimd", lambda e: e.memset(ones1[:, :], 1.0))
        E.op("gpsimd", lambda e: e.memset(indA[:, 0:64], 1.0))
        E.op("gpsimd", lambda e: e.memset(indA[:, 64:128], 0.0))
        E.op("gpsimd", lambda e: e.memset(indB[:, 0:64], 0.0))
        E.op("gpsimd", lambda e: e.memset(indB[:, 64:128], 1.0))
        ones_ap = bass.AP(v_aug, 64, [[NT * 1040, 128], [1040, NT], [65, H], [1, 1]])
        E.op("gpsimd", lambda e: e.memset(ones_ap, 1.0), inc="sg")
        sg_init = E.c("sg")

        E.op("vector", lambda e: e.tensor_tensor(identb[:, :], rowi[:, :], coli[:, :],
                                                 ALU.is_equal),
             waits=[("sg", sg_init)])
        E.op("vector", lambda e: e.tensor_tensor(trib[:, :], rowi[:, :], coli[:, :],
                                                 ALU.is_le), inc="sv")
        sv_consts = E.c("sv")

        E.op("gpsimd",
             lambda e: e.dma_start(bass.AP(biasT, 0, [[D, 1], [1, D]]),
                                   bass.AP(bias, 0, [[D, 1], [1, D]])),
             inc="sw", n=16)
        w_bias = E.c("sw")

        def ln_chain(par, src_ap, prev_w, sq_dump, sq_waits=(), sq_src=None):
            st_ = statsP[par]
            E.op("vector", lambda e, st_=st_, src_ap=src_ap: e.reduce_sum(
                st_[:, SUMX:SUMX + 1], src_ap, axis=AX),
                waits=[prev_w], inc="sv")
            v_sumx = E.c("sv")
            sq_src = src_ap if sq_src is None else sq_src
            E.op("scalar", lambda e, st_=st_, sq_src=sq_src, sq_dump=sq_dump: e.activation(
                sq_dump, sq_src, AF.Square, accum_out=st_[:, SUMSQ:SUMSQ + 1]),
                waits=[prev_w] + list(sq_waits), inc="ss")
            s_acc = E.c("ss")
            E.op("gpsimd", lambda e, st_=st_: e.tensor_scalar(
                st_[:, NEGMEAN:NEGMEAN + 1], st_[:, SUMX:SUMX + 1],
                -1.0 / D, None, ALU.mult),
                waits=[("ss", s_acc), ("sv", v_sumx)], inc="sg")
            E.wait("gpsimd", ("sg", E.c("sg")))
            E.op("gpsimd", lambda e, st_=st_: e.tensor_scalar(
                st_[:, SS_:SS_ + 1], st_[:, SUMSQ:SUMSQ + 1], 1.0 / D, None, ALU.mult),
                inc="sg")
            E.wait("gpsimd", ("sg", E.c("sg")))
            E.op("gpsimd", lambda e, st_=st_: e.tensor_tensor(
                st_[:, M2:M2 + 1], st_[:, NEGMEAN:NEGMEAN + 1],
                st_[:, NEGMEAN:NEGMEAN + 1], ALU.mult), inc="sg")
            E.wait("gpsimd", ("sg", E.c("sg")))
            E.op("gpsimd", lambda e, st_=st_: e.tensor_tensor(
                st_[:, VAR:VAR + 1], st_[:, SS_:SS_ + 1], st_[:, M2:M2 + 1],
                ALU.subtract), inc="sg")
            v_var = E.c("sg")
            E.op("scalar", lambda e, st_=st_: e.activation(
                st_[:, LNV:LNV + 1], st_[:, VAR:VAR + 1], AF.Ln,
                bias=cst[:, C_EPS:C_EPS + 1]),
                waits=[("sg", v_var)], inc="ss")
            E.wait("scalar", ("ss", E.c("ss")))
            E.op("scalar", lambda e, st_=st_: e.activation(
                st_[:, RSTD:RSTD + 1], st_[:, LNV:LNV + 1], AF.Exp,
                scale=cst[:, C_M05:C_M05 + 1]), inc="ss")
            E.wait("scalar", ("ss", E.c("ss")))
            E.op("scalar", lambda e, st_=st_: e.activation(
                st_[:, NEGMRS:NEGMRS + 1], st_[:, NEGMEAN:NEGMEAN + 1], AF.Identity,
                scale=st_[:, RSTD:RSTD + 1]), inc="ss")
            return E.c("ss")

        p34_done_st = 0
        p34_done_ss = 0
        stats_free = [("sv", 0)] * 4
        wd_spill_sw = 0
        total_out_dmas = 0

        for lb in range(BPC):
            with ExitStack() as es12:
                sb12 = lambda name, shape, dt: es12.enter_context(
                    nc.sbuf_tensor(name + f"_b{lb}", shape, dt))
                Wp = [sb12(f"Wp{p}", [128, 8 * D], BF16) for p in range(2)]
                h1f = sb12("h1f", [128, 8 * S], BF16)
                xin = [sb12(f"xin{p}", [128, D], F32) for p in range(4)]
                h1t = [sb12(f"h1t{p}", [128, D], BF16) for p in range(2)]
                wst = sb12("wst", [128, 512], F32)
                wab = sb12("wab", [128, 512], F32)
                wya = sb12("wya", [128, 512], F32)
                wm01 = sb12("wm01", [128, 512], F32)
                wq8 = sb12("wq8", [128, 512], FP8)
                wqd = [sb12(f"wqd{p}", [128, 512], BF16) for p in range(2)]
                wsgn = sb12("wsgn", [128, 512], BF16)
                wsc = sb12("wsc", [128, 16], F32)
                wrsc = sb12("wrsc", [128, 16], F32)
                wtst = sb12("wtst", [128, 512], BF16)

                def w_dma(src, dst, wait_st=0):
                    E.op("gpsimd",
                         lambda e, src=src, dst=dst: e.dma_start(
                             dst[:, :].rearrange("p (c m) -> p c m", m=D),
                             src.rearrange("(c p) m -> p c m", p=128)),
                         waits=[("st", wait_st)], inc="sw", n=16)
                    return E.c("sw")

                w_wq = w_dma(wq, Wp[0])
                w_wk = w_dma(wk, Wp[1])

                wd_state = {"sv": 0, "sabs": 0}
                wd_tr_st = [0, 0]
                wd_chain_sv = [0] * NT

                def wd_chain(sidx):
                    jc, kh = sidx // 2, sidx % 2
                    E.op("gpsimd", lambda e, jc=jc, kh=kh: e.dma_start(
                        wst[:, :],
                        bass.AP(wff, jc * 128 * D + kh * 512, [[D, 128], [1, 512]])),
                        waits=[("ss", wd_state["sabs"])], inc="sw", n=16)
                    ld = E.c("sw")
                    E.op("scalar", lambda e: e.activation(wab[:, :], wst[:, :], AF.Abs),
                         waits=[("sw", ld), ("sv", wd_state["sv"])])
                    E.op("scalar", lambda e: e.activation(wsgn[:, :], wst[:, :], AF.Sign),
                         inc="ss")
                    wd_state["sabs"] = E.c("ss")
                    def vedge():
                        E.wait("vector", ("sv", E.c("sv")))
                    E.op("vector", lambda e: e.reduce_max(
                        wsc[:, :], wab[:, :].rearrange("p (b k) -> p b k", k=32), axis=AX),
                        waits=[("ss", wd_state["sabs"])],
                        inc="sv")
                    vedge()
                    E.op("vector", lambda e: e.tensor_scalar(
                        wsc[:, :], wsc[:, :], 1.0 / 7.5, 1e-30, ALU.mult, ALU.max),
                        inc="sv")
                    vedge()
                    E.op("vector", lambda e: e.reciprocal(wrsc[:, :], wsc[:, :]), inc="sv")
                    vedge()
                    rsc_b = bass.AP(wrsc, 0, [[16, 128], [1, 16], [0, 32]])
                    sc_b = bass.AP(wsc, 0, [[16, 128], [1, 16], [0, 32]])
                    E.op("vector", lambda e, rsc_b=rsc_b: e.tensor_tensor(
                        wya[:, :].rearrange("p (b k) -> p b k", k=32),
                        wab[:, :].rearrange("p (b k) -> p b k", k=32), rsc_b, ALU.mult))
                    E.op("vector", lambda e: e.tensor_single_scalar(
                        wm01[:, :], wya[:, :], 1.0, ALU.is_lt))
                    E.op("vector", lambda e: e.tensor_tensor(
                        wya[:, :], wya[:, :], wm01[:, :], ALU.add))
                    E.op("vector", lambda e: e.tensor_copy(wq8[:, :], wya[:, :]))
                    wq_ = wqd[sidx % 2]
                    E.op("vector", lambda e, wq_=wq_: e.tensor_tensor(
                        wq_[:, :], wq8[:, :], wm01[:, :], ALU.subtract),
                        waits=[("st", wd_tr_st[sidx % 2])])
                    E.op("vector", lambda e, wq_=wq_, sc_b=sc_b: e.tensor_tensor(
                        wq_[:, :].rearrange("p (b k) -> p b k", k=32),
                        wq_[:, :].rearrange("p (b k) -> p b k", k=32), sc_b, ALU.mult))
                    E.op("vector", lambda e, wq_=wq_: e.tensor_tensor(
                        wq_[:, :], wq_[:, :], wsgn[:, :], ALU.mult), inc="sv")
                    wd_state["sv"] = E.c("sv")
                    wd_chain_sv[sidx] = E.c("sv")

                def wd_tail(sidx):
                    jc, kh = sidx // 2, sidx % 2
                    half = sidx % 2
                    wq_ = wqd[sidx % 2]
                    for c in range(4):
                        E.op("tensor", lambda e, c=c, half=half, wq_=wq_: e.transpose(
                            ptxv[:, half * 1024 + c * 128: half * 1024 + (c + 1) * 128],
                            wq_[:, c * 128:(c + 1) * 128], identb[:, :]),
                            waits=([("sv", wd_chain_sv[sidx]), ("sv", sv_consts),
                                    pstx_free[half]] if c == 0 else []),
                            inc="st" if c == 3 else None)
                    t_tr = E.c("st")
                    wd_tr_st[sidx % 2] = t_tr
                    E.op("vector", lambda e, half=half: e.tensor_copy(
                        wtst[:, :], ptxv[:, half * 1024: half * 1024 + 512]),
                        waits=[("st", t_tr), ("sp", wd_spill_guard[sidx])], inc="sv")
                    pstx_free[half] = ("sv", E.c("sv"))
                    s_ev = E.c("sv")
                    dest = bass.AP(wdTs, 4 * kh * D + jc * 128,
                                   [[8 * D, 128], [D, 4], [1, 128]])
                    E.op("sync", lambda e, dest=dest: e.dma_start(
                        dest, wtst[:, :].rearrange("p (c j) -> p c j", j=128)),
                        waits=[("sv", s_ev)], inc="sp", n=16)
                    return E.c("sp")

                wd_spill_guard = [0] * 17

                xin_free_sv = [0, 0, 0, 0]
                h1t_free_st = [p34_done_st, p34_done_st]
                sxw = 0
                p1_last_ss = 0
                h1f_ev = [0] * NT
                evac_free = [("ss", 0), ("sv", 0), ("sv", 0), ("sv", 0)]

                ln_nmrs = [0] * NT

                def p1_ln(t):
                    par = t % 4
                    E.op("sync", lambda e, lb=lb, t=t, par=par: e.dma_start(
                        xin[par][:, :], x_tile_ap(lb, t * 128)),
                        waits=[("sg", xin_free_sv[par])], inc="sx", n=16)
                    sx_t = E.c("sx")
                    E.wait("scalar", ("sx", sx_t))
                    E.wait("vector", ("sx", sx_t))
                    sq_dump = bass.AP(h1f, t * 128, [[8 * S, 128], [S, 8], [1, 128]])
                    ln_nmrs[t] = ln_chain(
                        par, xin[par][:, :], stats_free[par], sq_dump,
                        sq_src=xin[par][:, :].rearrange("p (c s) -> p c s", s=128))

                p1_ln(0)
                p1_ln(1)
                for t in range(NT):
                    par = t % 4
                    hpar = t % 2
                    s_nmrs = ln_nmrs[t]
                    stp = statsP[par]
                    E.op("gpsimd", lambda e, par=par, hpar=hpar, stp=stp: e.tensor_scalar(
                        h1t[hpar][:, :], xin[par][:, :],
                        stp[:, RSTD:RSTD + 1], stp[:, NEGMRS:NEGMRS + 1],
                        ALU.mult, ALU.add),
                        waits=[("ss", s_nmrs), ("st", h1t_free_st[hpar])], inc="sg")
                    v_h1 = E.c("sg")
                    xin_free_sv[par] = v_h1
                    stats_free[par] = ("sg", v_h1)
                    if t + 2 < NT:
                        p1_ln(t + 2)
                    if t >= 8:
                        mo, sblk = (t - 8) // 4, ((t - 8) // 2) % 2
                        bank = mo * 2 + sblk
                        for mc in range(4 * (t % 2), 4 * (t % 2) + 4):
                            E.op("tensor", lambda e, mo=mo, mc=mc, sblk=sblk, bank=bank: e.matmul(
                                psS[:, bank * 512:(bank + 1) * 512],
                                Wp[0][:, mc * D + mo * 128: mc * D + (mo + 1) * 128],
                                h1f[:, mc * S + sblk * 512: mc * S + sblk * 512 + 512],
                                start=(mc == 0), stop=(mc == 7)),
                                waits=([("sw", w_wq),
                                        ("sv", h1f_ev[4 * sblk + 3])]
                                       if mc == 0 else []),
                                inc="st" if mc == 7 else None)
                    half = hpar
                    for c in range(8):
                        E.op("tensor", lambda e, c=c, half=half, hpar=hpar: e.transpose(
                            ptxv[:, half * 1024 + c * 128: half * 1024 + (c + 1) * 128],
                            h1t[hpar][:, c * 128:(c + 1) * 128], identb[:, :]),
                            waits=([("sg", v_h1), ("sv", sv_consts),
                                    pstx_free[half]] if c == 0 else []),
                            inc="st" if c == 7 else None)
                    t_tr = E.c("st")
                    h1t_free_st[hpar] = t_tr
                    dest = bass.AP(h1f, t * 128, [[8 * S, 128], [S, 8], [1, 128]])
                    E.op("vector", lambda e, dest=dest, half=half: e.tensor_copy(
                        dest, ptxv[:, half * 1024: half * 1024 + 1024].rearrange('p (c s) -> p c s', s=128)),
                        waits=[("st", t_tr)], inc="sv")
                    pstx_free[half] = ("sv", E.c("sv"))
                    p1_last_ss = E.c("sv")
                    h1f_ev[t] = E.c("sv")
                h1_ready = p1_last_ss
                t_mm01 = E.c("st")
                for mo in (0, 1):
                    for sblk in (0, 1):
                        bank = mo * 2 + sblk
                        E.op("vector", lambda e, mo=mo, sblk=sblk, bank=bank: e.tensor_copy(
                            qT[:, mo * S + sblk * 512: mo * S + sblk * 512 + 512],
                            psS[:, bank * 512:(bank + 1) * 512]),
                            waits=[("st", t_mm01)] if bank == 0 else [], inc="sv")
                        evac_free[bank] = ("sv", E.c("sv"))

                qT_done_st = 0

                for phase, (wslot, dstT, wsem) in enumerate(
                        ((0, qT, w_wq), (1, kT, w_wk))):
                    for mo in range(8):
                        sblks = ((2, 3) if mo < 2 else (0, 1, 2, 3)) \
                            if phase == 0 else (0, 1, 2, 3)
                        for sblk in sblks:
                            for mc in range(8):
                                E.op("tensor", lambda e, wslot=wslot, mo=mo, mc=mc, sblk=sblk: e.matmul(
                                    psS[:, sblk * 512:(sblk + 1) * 512],
                                    Wp[wslot][:, mc * D + mo * 128: mc * D + (mo + 1) * 128],
                                    h1f[:, mc * S + sblk * 512: mc * S + sblk * 512 + 512],
                                    start=(mc == 0), stop=(mc == 7)),
                                    waits=([("sw", wsem),
                                            ("sv", h1f_ev[4 * sblk + 3]),
                                            evac_free[sblk]]
                                           if mc == 0 else []),
                                    inc="st" if (mc == 7 and sblk == sblks[-1]) else None)
                        t_mm = E.c("st")
                        for sblk in sblks:
                            E.op("vector", lambda e, mo=mo, sblk=sblk, dstT=dstT: e.tensor_copy(
                                dstT[:, mo * S + sblk * 512: mo * S + sblk * 512 + 512],
                                psS[:, sblk * 512:(sblk + 1) * 512]),
                                waits=[("st", t_mm)] if sblk == sblks[0] else [], inc="sv")
                            evac_free[sblk] = ("sv", E.c("sv"))
                        if lb == 0:
                            k = phase * 8 + mo
                            wd_chain(k)
                            if k > 0:
                                wd_spill_guard[k] = wd_tail(k - 1)
                    if phase == 0:
                        qT_done_st = E.c("st")
                        w_wv = w_dma(wv, Wp[0], wait_st=qT_done_st)

                for t in range(NT):
                    pp = t % 2
                    pofs = pp * 1024
                    for mc in range(8):
                        for nb in range(2):
                            E.op("tensor", lambda e, mc=mc, nb=nb, t=t, pofs=pofs: e.matmul(
                                psS[:, pofs + nb * 512: pofs + nb * 512 + 512],
                                h1f[:, mc * S + t * 128: mc * S + (t + 1) * 128],
                                Wp[0][:, mc * D + nb * 512: mc * D + nb * 512 + 512],
                                start=(mc == 0), stop=(mc == 7)),
                                waits=([("sw", w_wv), evac_free[2 * pp],
                                        evac_free[2 * pp + 1]]
                                       if (mc == 0 and nb == 0) else []),
                                inc="st" if (mc == 7 and nb == 1) else None)
                    t_mm = E.c("st")
                    dest = bass.AP(v_aug, t * 1040, [[NT * 1040, 128], [65, 16], [1, 64]])
                    E.op("vector", lambda e, dest=dest, pofs=pofs: e.tensor_copy(
                        dest, bass.AP(psS, pofs, [[2048, 128], [64, 16], [1, 64]])),
                        waits=[("st", t_mm), ("sg", sg_init)], inc="sv")
                    evac_free[2 * pp] = ("sv", E.c("sv"))
                    evac_free[2 * pp + 1] = ("sv", E.c("sv"))

                if debug and lb == 0:
                    E.op("sync", lambda e: e.dma_start(dbg["d_h1f"][:, :], h1f[:, :]),
                         waits=[("sv", h1_ready)], inc="sp", n=16)
                    E.op("sync", lambda e: e.dma_start(dbg["d_qT"][:, :], qT[:, :]),
                         waits=[("sv", E.c("sv"))], inc="sp", n=16)
                    E.op("sync", lambda e: e.dma_start(dbg["d_kT"][:, :], kT[:, :]),
                         waits=[("sv", E.c("sv"))], inc="sp", n=16)
                    E.op("sync", lambda e: e.dma_start(dbg["d_vaug"][:, :], v_aug[:, :]),
                         waits=[("sv", E.c("sv"))], inc="sp", n=16)
                    E.wait("sync", ("sp", E.c("sp")))
                if lb == 0:
                    wd_spill_sw = wd_tail(NT - 1)
                p2_done_st = E.c("st")
                p2_done_ss = E.c("ss")
                p2_done_sv = E.c("sv")
                E.flush()

            with ExitStack() as es34:
                sb34 = lambda name, shape, dt: es34.enter_context(
                    nc.sbuf_tensor(name + f"_b{lb}", shape, dt))
                Wo_sb = sb34("Wo_sb", [128, 8 * D], BF16)
                wdT = sb34("wdT", [128, 8 * D], BF16)
                e_sb = sb34("e_sb", [128, 4 * 1024], BF16)
                out_norm = sb34("out_norm", [128, 8 * 512], BF16)
                xr = sb34("xr", [128, 4 * D], F32)
                r2buf = sb34("r2buf", [128, 2 * D], F32)
                R_sb = sb34("R_sb", [128, 1024], BF16)
                rbAB_b = sb34("rbAB_b", [1, 1024], BF16)
                ah = sb34("ah", [128, D], F32)
                m01 = sb34("m01", [128, D], BF16)
                q8t = sb34("q8t", [128, D], FP8)
                sgn = sb34("sgn", [128, D], BF16)
                qa = sb34("qa", [128, D], BF16)
                xd_blk = sb34("xd_blk", [128, 4 * D], BF16)
                xdT = [sb34(f"xdT{p}", [128, D], BF16) for p in range(2)]
                bmax = sb34("bmax", [128, 32], F32)
                scl = sb34("scl", [128, 32], F32)
                rscl = sb34("rscl", [128, 32], F32)
                qset0 = dict(ah=ah, sgn=sgn, m01=m01, q8t=q8t, qa=qa,
                             bmax=bmax, scl=scl, rscl=rscl, free_sv=[0],
                             init_st=[0])
                qset1 = dict(ah=e_sb[:, 0:2048].bitcast(F32),
                             sgn=e_sb[:, 2048:3072],
                             m01=e_sb[:, 3072:4096],
                             q8t=out_norm[:, 0:512].bitcast(FP8),
                             qa=out_norm[:, 512:1536],
                             bmax=R_sb[:, 0:64].bitcast(F32),
                             scl=R_sb[:, 64:128].bitcast(F32),
                             rscl=R_sb[:, 128:192].bitcast(F32),
                             free_sv=[0], init_st=[0])

                E.op("gpsimd", lambda e: e.dma_start(
                    Wo_sb[:, :].rearrange("p (c m) -> p c m", m=D),
                    wo.rearrange("(c p) m -> p c m", p=128)),
                    inc="sw", n=16)
                w_wo = E.c("sw")
                E.op("gpsimd", lambda e: e.dma_start(wdT[:, :], wdTs[:, :]),
                     waits=[("sp", wd_spill_sw)], inc="sw", n=16)
                w_wdT = E.c("sw")

                slot_free_st = [p2_done_st] * 4
                pair_half_free = [p2_done_ss, p2_done_ss]
                acc_free2 = [0, 0]
                ps_guard_sv = 0
                r2_free_so = [0, 0]
                out_norm_free_st = p2_done_st
                R_guard_sv2 = [0, 0]
                xd_free_ss = p2_done_ss
                xdT_free_st = [p2_done_st, p2_done_st]
                rb_free_P = [0, 0]

                E.op("sync", lambda e, lb=lb: e.dma_start(
                    xr[:, :].rearrange("p (t m) -> p t m", m=D), x_block_ap(lb, 0, 4)),
                    inc="sxr", n=16)
                xr_loaded = [E.c("sxr")] * 4
                xr_read_ss = [0, 0, 0, 0]

                psW_free_sv = [0, 0]

                ffn_vxd = [0, 0, 0, 0]
                r1_ready_sv = [0, 0, 0, 0]
                xdblk_free_st = p2_done_st
                norm_hold = [0]

                def emit_wo_resid(i):
                    nonlocal out_norm_free_st, ps_guard_sv
                    norm_done_sv = norm_hold[0]
                    for qs in range(4):
                        pp = qs % 2
                        pofs = pp * 1024
                        for mc in range(8):
                            for nb in range(2):
                                E.op("tensor", lambda e, mc=mc, nb=nb, qs=qs, pofs=pofs: e.matmul(
                                    psS[:, pofs + nb * 512: pofs + nb * 512 + 512],
                                    out_norm[:, mc * 512 + qs * 128: mc * 512 + (qs + 1) * 128],
                                    Wo_sb[:, mc * D + nb * 512: mc * D + nb * 512 + 512],
                                    start=(mc == 0), stop=(mc == 7)),
                                    waits=([("sv", norm_done_sv), ("sw", w_wo),
                                            ("sv", psW_free_sv[pp])]
                                           if (mc == 0 and nb == 0) else []),
                                    inc="st" if (mc == 7 and nb == 1) else None)
                        t_wo = E.c("st")
                        E.op("vector", lambda e, qs=qs, pofs=pofs: e.tensor_tensor(
                            xr[:, qs * D:(qs + 1) * D], xr[:, qs * D:(qs + 1) * D],
                            psS[:, pofs:pofs + 1024], ALU.add),
                            waits=[("st", t_wo), ("sxr", xr_loaded[qs])], inc="sv")
                        psW_free_sv[pp] = E.c("sv")
                        r1_ready_sv[qs] = E.c("sv")
                    out_norm_free_st = E.c("st")
                    ps_guard_sv = max(psW_free_sv)
                    if debug and lb == 0 and i == 0:
                        E.op("sync", lambda e: e.dma_start(dbg["d_r1"][:, :], xr[:, :]),
                             waits=[("sv", r1_ready_sv[3])], inc="sp", n=16)
                        E.wait("sync", ("sp", E.c("sp")))

                def emit_pre_a(i, qs, B=qset0):
                    xr_t = xr[:, qs * D:(qs + 1) * D]
                    st_ = statsP[qs]
                    E.op("vector", lambda e, st_=st_, xr_t=xr_t: e.reduce_sum(
                        st_[:, SUMX:SUMX + 1], xr_t, axis=AX),
                        waits=[stats_free[qs]], inc="sv")
                    ah = B["ah"]
                    E.op("gpsimd", lambda e, xr_t=xr_t, ah=ah: e.tensor_tensor(
                        ah[:, :], xr_t, xr_t, ALU.mult),
                        waits=[("sv", B["free_sv"][0]), ("st", B["init_st"][0]),
                               ("sv", r1_ready_sv[qs])], inc="sg")
                    g_sq = E.c("sg")
                    E.op("vector", lambda e, st_=st_, ah=ah: e.reduce_sum(
                        st_[:, SUMSQ:SUMSQ + 1], ah[:, :], axis=AX),
                        waits=[("sg", g_sq)], inc="sv")
                    v_ssq = E.c("sv")
                    E.op("gpsimd", lambda e, st_=st_: e.tensor_scalar(
                        st_[:, NEGMEAN:NEGMEAN + 1], st_[:, SUMX:SUMX + 1],
                        -1.0 / D, None, ALU.mult),
                        waits=[("sv", v_ssq)], inc="sg")
                    E.wait("gpsimd", ("sg", E.c("sg")))
                    E.op("gpsimd", lambda e, st_=st_: e.tensor_scalar(
                        st_[:, SS_:SS_ + 1], st_[:, SUMSQ:SUMSQ + 1],
                        1.0 / D, None, ALU.mult), inc="sg")
                    E.wait("gpsimd", ("sg", E.c("sg")))
                    E.op("gpsimd", lambda e, st_=st_: e.tensor_tensor(
                        st_[:, M2:M2 + 1], st_[:, NEGMEAN:NEGMEAN + 1],
                        st_[:, NEGMEAN:NEGMEAN + 1], ALU.mult), inc="sg")
                    E.wait("gpsimd", ("sg", E.c("sg")))
                    E.op("gpsimd", lambda e, st_=st_: e.tensor_tensor(
                        st_[:, VAR:VAR + 1], st_[:, SS_:SS_ + 1], st_[:, M2:M2 + 1],
                        ALU.subtract), inc="sg")
                    v_var = E.c("sg")
                    E.op("scalar", lambda e, st_=st_: e.activation(
                        st_[:, LNV:LNV + 1], st_[:, VAR:VAR + 1], AF.Ln,
                        bias=cst[:, C_EPS:C_EPS + 1]),
                        waits=[("sg", v_var)])
                    E.op("scalar", lambda e, st_=st_: e.activation(
                        st_[:, RSTD:RSTD + 1], st_[:, LNV:LNV + 1], AF.Exp,
                        scale=cst[:, C_M05:C_M05 + 1]), inc="ss")
                    return (E.c("ss"), v_ssq)

                def emit_pre_b(i, qs, ab_, B=qset0):
                    s_rstd, v_ssq = ab_
                    xr_t = xr[:, qs * D:(qs + 1) * D]
                    stp = statsP[qs]
                    E.op("gpsimd", lambda e, stp=stp: e.tensor_tensor(
                        stp[:, NEGMRS:NEGMRS + 1], stp[:, NEGMEAN:NEGMEAN + 1],
                        stp[:, RSTD:RSTD + 1], ALU.mult),
                        waits=[("ss", s_rstd)], inc="sg")
                    E.wait("gpsimd", ("sg", E.c("sg")))
                    ah, sgn = B["ah"], B["sgn"]
                    E.op("gpsimd", lambda e, xr_t=xr_t, stp=stp, ah=ah: e.tensor_scalar(
                        ah[:, :], xr_t, stp[:, RSTD:RSTD + 1],
                        stp[:, NEGMRS:NEGMRS + 1], ALU.mult, ALU.add),
                        waits=[("sv", v_ssq)], inc="sg")
                    g_t = E.c("sg")
                    xr_read_ss[qs] = g_t
                    stats_free[qs] = ("sg", g_t)
                    E.op("vector", lambda e, ah=ah, sgn=sgn: e.tensor_single_scalar(
                        sgn[:, :], ah[:, :], 0.0, ALU.is_ge),
                        waits=[("sg", g_t)])
                    E.op("vector", lambda e, sgn=sgn: e.tensor_scalar(
                        sgn[:, :], sgn[:, :], 2.0, -1.0, ALU.mult, ALU.add))
                    E.op("vector", lambda e, ah=ah, sgn=sgn: e.tensor_tensor(
                        ah[:, :], ah[:, :], sgn[:, :], ALU.mult), inc="sv")
                    return E.c("sv")

                def emit_pre_c(i, qs, v_abs, B=None):
                    if B is None:
                        B = qset0
                    ah, sgn, m01 = B["ah"], B["sgn"], B["m01"]
                    q8t, qa = B["q8t"], B["qa"]
                    bmax, scl, rscl = B["bmax"], B["scl"], B["rscl"]
                    if True:
                        E.op("vector", lambda e, bmax=bmax, ah=ah: e.reduce_max(
                            bmax[:, :], ah[:, :].rearrange("p (b k) -> p b k", k=32),
                            axis=AX), inc="sv")
                        E.wait("vector", ("sv", E.c("sv")))
                        E.op("vector", lambda e, scl=scl, bmax=bmax: e.tensor_scalar(
                            scl[:, :], bmax[:, :], 1.0 / 7.5, 1e-30, ALU.mult, ALU.max),
                            inc="sv")
                        E.wait("vector", ("sv", E.c("sv")))
                        E.op("vector", lambda e, rscl=rscl, scl=scl: e.reciprocal(
                            rscl[:, :], scl[:, :]), inc="sv")
                        E.wait("vector", ("sv", E.c("sv")))
                        ah3 = ah[:, :].rearrange("p (b k) -> p b k", k=32)
                        rsc_b = rscl[:, :].rearrange(
                            "p (b o) -> p b o", o=1).broadcast_to([128, 32, 32])
                        sc_b = scl[:, :].rearrange(
                            "p (b o) -> p b o", o=1).broadcast_to([128, 32, 32])
                        E.op("vector", lambda e, rsc_b=rsc_b, ah3=ah3: e.tensor_tensor(
                            ah3, ah3, rsc_b, ALU.mult))
                        E.op("vector", lambda e, m01=m01, ah=ah: e.tensor_single_scalar(
                            m01[:, :], ah[:, :], 1.0, ALU.is_lt))
                        E.op("vector", lambda e, ah=ah, m01=m01: e.tensor_tensor(
                            ah[:, :], ah[:, :], m01[:, :], ALU.add))
                        E.op("vector", lambda e, q8t=q8t, ah=ah: e.tensor_copy(
                            q8t[:, :], ah[:, :]))
                        E.op("vector", lambda e, qa=qa, q8t=q8t, m01=m01: e.tensor_tensor(
                            qa[:, :], q8t[:, :], m01[:, :], ALU.subtract))
                        qa3 = qa[:, :].rearrange("p (b k) -> p b k", k=32)
                        E.op("vector", lambda e, sc_b=sc_b, qa3=qa3: e.tensor_tensor(
                            qa3, qa3, sc_b, ALU.mult))
                        E.op("vector", lambda e, qs=qs, qa=qa, sgn=sgn: e.tensor_tensor(
                            xd_blk[:, qs * D:(qs + 1) * D], qa[:, :], sgn[:, :], ALU.mult),
                            waits=[("st", xdblk_free_st)], inc="sv")
                        ffn_vxd[qs] = E.c("sv")
                        B["free_sv"][0] = ffn_vxd[qs]
                        if debug and lb == 0 and i == 0 and qs == 0:
                            E.op("sync", lambda e: e.dma_start(
                                dbg["d_xd"][:, :], xd_blk[:, 0:D]),
                                waits=[("sv", ffn_vxd[0])], inc="sp", n=16)
                            E.wait("sync", ("sp", E.c("sp")))

                def emit_ffn_pre_qs(i, qs, B=None):
                    if B is None:
                        B = qset0
                    s_nmrs = emit_pre_a(i, qs, B)
                    s_ah = emit_pre_b(i, qs, s_nmrs, B)
                    emit_pre_c(i, qs, s_ah, B)

                def emit_ffn_post(i):
                    nonlocal xdblk_free_st, ps_guard_sv
                    for qs in range(4):
                        emit_ffn_post_qs(i, qs)
                    xdblk_free_st = E.c("st")
                    ps_guard_sv = max(psW_free_sv)

                def emit_ffn_post_qs(i, qs):
                    nonlocal total_out_dmas
                    if True:
                        pp2 = qs % 2
                        pofs = pp2 * 1024
                        for c in range(8):
                            E.op("tensor", lambda e, c=c, pp2=pp2, qs=qs: e.transpose(
                                ptxv[:, pp2 * 1024 + c * 128: pp2 * 1024 + (c + 1) * 128],
                                xd_blk[:, qs * D + c * 128: qs * D + (c + 1) * 128],
                                identb[:, :]),
                                waits=([("sv", ffn_vxd[qs]), pstx_free[pp2],
                                        ("sv", acc_free2[1])]
                                       if c == 0 else []),
                                inc="st" if c == 7 else None)
                        t_tx = E.c("st")
                        E.op("scalar", lambda e, pp2=pp2: e.copy(
                            xdT[pp2][:, :], ptxv[:, pp2 * 1024:(pp2 + 1) * 1024]),
                            waits=[("st", t_tx), ("st", xdT_free_st[pp2])], inc="ss")
                        s_xdT = E.c("ss")
                        pstx_free[pp2] = ("ss", s_xdT)
                        for nb in range(2):
                            E.op("tensor", lambda e, nb=nb, pofs=pofs: e.matmul(
                                psS[:, pofs + nb * 512: pofs + nb * 512 + 512],
                                ones1[0:1, :], biasT[0:1, nb * 512:(nb + 1) * 512],
                                start=True, stop=False),
                                waits=([("ss", s_xdT), ("sv", psW_free_sv[pp2]),
                                        ("sw", w_bias), ("sw", w_wdT), ("sg", sg_init),
                                        ("ss", pair_half_free[0]), ("ss", pair_half_free[1]),
                                        ("sv", R_guard_sv2[0]), ("sv", R_guard_sv2[1])]
                                       if nb == 0 else []))
                        for kc in range(8):
                            for nb in range(2):
                                E.op("tensor", lambda e, kc=kc, nb=nb, pp2=pp2, pofs=pofs: e.matmul(
                                    psS[:, pofs + nb * 512: pofs + nb * 512 + 512],
                                    xdT[pp2][:, kc * 128:(kc + 1) * 128],
                                    wdT[:, kc * D + nb * 512: kc * D + nb * 512 + 512],
                                    start=False, stop=(kc == 7)),
                                    inc="st" if (kc == 7 and nb == 1) else None)
                        t_ffn = E.c("st")
                        xdT_free_st[pp2] = t_ffn
                        if debug and lb == 0 and i == 0 and qs == 3:
                            E.op("scalar", lambda e, pofs=pofs: e.copy(
                                ah[:, :], psS[:, pofs:pofs + 1024]),
                                waits=[("st", t_ffn)], inc="ss")
                            s_dbg = E.c("ss")
                            E.op("sync", lambda e: e.dma_start(dbg["d_ps"][:, :], ah[:, :]),
                                 waits=[("ss", s_dbg)], inc="sp", n=16)
                            E.op("sync", lambda e, pp2=pp2: e.dma_start(
                                dbg["d_xdT"][:, :], xdT[pp2][:, :]),
                                waits=[("ss", s_dbg)], inc="sp", n=16)
                            E.wait("sync", ("sp", E.c("sp")))
                        E.op("vector", lambda e, qs=qs, pp2=pp2, pofs=pofs: e.tensor_tensor(
                            r2buf[:, pp2 * D:(pp2 + 1) * D], xr[:, qs * D:(qs + 1) * D],
                            psS[:, pofs:pofs + 1024], ALU.add),
                            waits=[("st", t_ffn), ("so", r2_free_so[pp2])], inc="sv")
                        v_r2 = E.c("sv")
                        psW_free_sv[pp2] = v_r2
                        E.op("sync", lambda e, lb=lb, i=i, qs=qs, pp2=pp2: e.dma_start(
                            out_tile_ap(lb, (i * 4 + qs) * 128),
                            r2buf[:, pp2 * D:(pp2 + 1) * D]),
                            waits=[("sv", v_r2)], inc="so", n=16)
                        r2_free_so[pp2] = E.c("so")
                        total_out_dmas += 1
                        if i + 1 < NI:
                            E.op("sync", lambda e, lb=lb, i=i, qs=qs: e.dma_start(
                                xr[:, qs * D:(qs + 1) * D],
                                x_tile_ap(lb, ((i + 1) * 4 + qs) * 128)),
                                waits=[("sv", v_r2), ("sg", xr_read_ss[qs])],
                                inc="sxr", n=16)
                            xr_loaded[qs] = E.c("sxr")

                pend_pre = [None]
                for i in range(NI):
                    njt = 4 * i + 4
                    nfull = 4 * i
                    for hpg in range(4):
                        exp_of = [{}, {}]

                        def emit_score2(P, jt):
                            hp = 2 * hpg + P
                            d = jt - nfull
                            full = d < 0
                            Nd = 512 if full else 512 - 128 * d
                            qo = i * 512 + (0 if full else 128 * d)
                            for p0, base in ((0, 0), (64, 1024)):
                                E.op("tensor", lambda e, p0=p0, base=base, P=P, hp=hp, jt=jt, qo=qo, Nd=Nd: e.matmul(
                                    psS[:, base + P * 512: base + P * 512 + Nd],
                                    kT[p0:p0 + 64, hp * S + jt * 128: hp * S + (jt + 1) * 128],
                                    qT[p0:p0 + 64, hp * S + qo: hp * S + qo + Nd],
                                    start=True, stop=True),
                                    waits=[("ss", pair_half_free[P]), ("sv", ps_guard_sv),
                                           ("sv", R_guard_sv2[P])],
                                    inc="st" if base == 1024 else None)
                            return E.c("st")

                        def emit_exp2(P, jt, t_sc):
                            d = jt - nfull
                            full = d < 0
                            Nd = 512 if full else 512 - 128 * d
                            slot = P * 2 + jt % 2
                            src = bass.AP(psS, P * 512, [[2048, 128], [1024, 2], [1, Nd]])
                            dst = bass.AP(e_sb, slot * 1024, [[4096, 128], [512, 2], [1, Nd]])
                            E.op("scalar", lambda e, src=src, dst=dst: e.activation(
                                dst, src, AF.Exp, bias=cst[:, C_EBI:C_EBI + 1],
                                scale=cst[:, C_ESC:C_ESC + 1]),
                                waits=[("st", t_sc), ("st", slot_free_st[slot]),
                                       ("sg", sg_init)],
                                inc="ss")
                            sdone = E.c("ss")
                            pair_half_free[P] = sdone
                            mask_v = 0
                            if not full:
                                mdst = bass.AP(e_sb, slot * 1024,
                                               [[4096, 128], [512, 2], [1, 128]])
                                msrc = bass.AP(trib, 0, [[128, 128], [0, 2], [1, 128]])
                                E.op("vector", lambda e, mdst=mdst, msrc=msrc: e.tensor_tensor(
                                    mdst, mdst, msrc, ALU.mult),
                                    waits=[("ss", sdone)], inc="sv")
                                mask_v = E.c("sv")
                            exp_of[P][jt] = (sdone, mask_v)

                        def emit_av2(P, jt):
                            hp = 2 * hpg + P
                            d = jt - nfull
                            full = d < 0
                            Nd = 512 if full else 512 - 128 * d
                            col0 = 0 if full else 128 * d
                            slot = P * 2 + jt % 2
                            acc = accAB if P == 0 else pAux
                            sdone, mask_v = exp_of[P][jt]
                            for aofs, hofs in ((0, 0), (512, 512)):
                                hh = 2 * hp + (0 if aofs == 0 else 1)
                                extra = ([pstx_free[0], pstx_free[1]]
                                         if (P == 1 and jt == 0) else [])
                                E.op("tensor", lambda e, jt=jt, hh=hh, acc=acc, aofs=aofs, col0=col0, Nd=Nd, slot=slot, hofs=hofs, njt=njt: e.matmul(
                                    acc[0:65, aofs + col0: aofs + col0 + Nd],
                                    v_aug[:, jt * 1040 + hh * 65: jt * 1040 + hh * 65 + 65],
                                    e_sb[:, slot * 1024 + hofs: slot * 1024 + hofs + Nd],
                                    start=(jt == 0), stop=(jt == njt - 1)),
                                    waits=[("ss", sdone), ("sv", mask_v),
                                           ("sv", acc_free2[P])] + extra,
                                    inc="st" if aofs == 512 else None)
                            slot_free_st[slot] = E.c("st")

                        t_sc0 = emit_score2(0, 0)
                        emit_exp2(0, 0, t_sc0)
                        t_sc1 = emit_score2(1, 0)
                        emit_exp2(1, 0, t_sc1)
                        chst = {}
                        for jt in range(njt):
                            for P in range(2):
                                if jt + 1 < njt:
                                    t_sc = emit_score2(P, jt + 1)
                                    emit_exp2(P, jt + 1, t_sc)
                                emit_av2(P, jt)
                            if pend_pre[0] is not None:
                                if jt == 1:
                                    chst["nmrs"] = emit_pre_a(pend_pre[0], hpg)
                                elif jt == 3:
                                    chst["ah"] = emit_pre_b(
                                        pend_pre[0], hpg, chst["nmrs"])
                                elif nfull >= 8 and jt == nfull - 4:
                                    emit_pre_c(pend_pre[0], hpg, chst["ah"])
                                    chst["c_done"] = True
                        av_done_st = E.c("st")

                        for P in range(2):
                            hp = 2 * hpg + P
                            acc = accAB if P == 0 else pAux
                            lnrow = (accAB if P == 0 else pAux)[96:97, 0:1024]
                            rbP = rbAB if P == 0 else rbAB_b
                            E.op("scalar", lambda e, acc=acc, lnrow=lnrow: e.activation(
                                lnrow, acc[64:65, 0:1024], AF.Ln),
                                waits=[("st", av_done_st)])
                            E.op("scalar", lambda e, lnrow=lnrow, rbP=rbP: e.activation(
                                rbP[0:1, 0:1024], lnrow, AF.Exp,
                                scale=cst[0:1, C_M1:C_M1 + 1]),
                                waits=[("st", rb_free_P[P])], inc="ss")
                            s_rb = E.c("ss")
                            E.op("tensor", lambda e, P=P, rbP=rbP: e.matmul(
                                psS[:, P * 512:(P + 1) * 512], indA[0:1, :],
                                rbP[0:1, 0:512],
                                start=True, stop=False),
                                waits=[("ss", s_rb), ("ss", pair_half_free[P]),
                                       ("sg", sg_init)])
                            E.op("tensor", lambda e, P=P, rbP=rbP: e.matmul(
                                psS[:, P * 512:(P + 1) * 512], indB[0:1, :],
                                rbP[0:1, 512:1024],
                                start=False, stop=True), inc="st")
                            t_R = E.c("st")
                            rb_free_P[P] = t_R
                            E.op("vector", lambda e, P=P: e.tensor_copy(
                                R_sb[:, P * 512:(P + 1) * 512],
                                psS[:, P * 512:(P + 1) * 512]),
                                waits=[("st", t_R), ("sv", acc_free2[P])], inc="sv")
                            s_R = E.c("sv")
                            R_guard_sv2[P] = s_R
                            E.op("vector", lambda e, hp=hp, P=P, acc=acc: e.tensor_tensor(
                                out_norm[0:64, hp * 512:(hp + 1) * 512],
                                acc[0:64, 0:512], R_sb[0:64, P * 512:(P + 1) * 512],
                                ALU.mult),
                                waits=[("sv", s_R), ("st", out_norm_free_st)])
                            E.op("vector", lambda e, hp=hp, P=P, acc=acc: e.tensor_tensor(
                                out_norm[64:128, hp * 512:(hp + 1) * 512],
                                acc[0:64, 512:1024],
                                R_sb[64:128, P * 512:(P + 1) * 512], ALU.mult),
                                inc="sv")
                            acc_free2[P] = E.c("sv")
                        if pend_pre[0] is not None and not chst.get("c_done"):
                            emit_pre_c(pend_pre[0], hpg, chst["ah"])
                    norm_hold[0] = E.c("sv")
                    if debug and lb == 0 and i == 0:
                        E.op("sync", lambda e: e.dma_start(dbg["d_onorm"][:, :], out_norm[:, :]),
                             waits=[("sv", norm_hold[0])], inc="sp", n=16)
                        E.op("sync", lambda e: e.dma_start(dbg["d_wdT"][:, :], wdT[:, :]),
                             waits=[("sw", w_wdT)], inc="sp", n=16)
                        E.wait("sync", ("sp", E.c("sp")))

                    if DEFER_FFN and i > 0:
                        emit_ffn_post(i - 1)
                    emit_wo_resid(i)
                    if DEFER_FFN:
                        pend_pre[0] = i
                    else:
                        for qs in range(4):
                            emit_ffn_pre_qs(i, qs)
                        emit_ffn_post(i)

                if DEFER_FFN:
                    qset1["init_st"][0] = E.c("st")
                    qset1["free_sv"][0] = 0
                    for qs in range(4):
                        emit_ffn_pre_qs(NI - 1, qs, qset1 if qs % 2 else qset0)
                        emit_ffn_post_qs(NI - 1, qs)
                p34_done_st = E.c("st")
                p34_done_ss = E.c("ss")
                E.flush()

        E.wait("sync", ("so", E.c("so")))
        E.flush()

    return nc


_NC_CACHE = None


def kernel(L0_attn_x, L0_attn_Wq, L0_attn_Wk, L0_attn_Wv, L0_attn_Wo,
           L0_ffn_weight, L0_ffn_bias):
    global _NC_CACHE
    xh = np.ascontiguousarray(np.asarray(L0_attn_x, dtype=np.float32))
    ins = {
        "wq": np.ascontiguousarray(np.asarray(L0_attn_Wq, dtype=np.float32)),
        "wk": np.ascontiguousarray(np.asarray(L0_attn_Wk, dtype=np.float32)),
        "wv": np.ascontiguousarray(np.asarray(L0_attn_Wv, dtype=np.float32)),
        "wo": np.ascontiguousarray(np.asarray(L0_attn_Wo, dtype=np.float32)),
        "wff": np.ascontiguousarray(np.asarray(L0_ffn_weight, dtype=np.float32)),
        "bias": np.ascontiguousarray(np.asarray(L0_ffn_bias, dtype=np.float32)),
    }
    if _NC_CACHE is None:
        _NC_CACHE = build_nc()
    in_maps = [dict(ins, x=xh[c * BPC:(c + 1) * BPC]) for c in range(8)]
    res = run_bass_kernel_spmd(_NC_CACHE, in_maps, core_ids=list(range(8)))
    return np.concatenate([r["out"] for r in res.results], axis=0)


# version 30
# speedup vs baseline: 1.2135x; 1.2135x over previous
from contextlib import ExitStack
import numpy as np

import concourse.bass as bass
import concourse.mybir as mybir
from concourse.bass_utils import run_bass_kernel_spmd

F32 = mybir.dt.float32
BF16 = mybir.dt.bfloat16
FP8 = mybir.dt.float8e4
AF = mybir.ActivationFunctionType
ALU = mybir.AluOpType
AX = mybir.AxisListType.X

S = 2048
D = 1024
H = 16
HD = 64
BPC = 2
NT = 16
NI = 4

SUMX, SUMSQ, NEGMEAN, SS_, M2, VAR, LNV, RSTD, NEGMRS = range(9)
C_EPS, C_M05, C_ESC, C_EBI, C_M1 = range(5)

ENGINES = ("sync", "gpsimd", "scalar", "vector", "tensor")
import os as _os
DEFER_FFN = _os.environ.get("DEFER_FFN", "1") == "1"


class Em:

    def __init__(self, nc, es):
        self.nc = nc
        self.q = {e: [] for e in ENGINES}
        self.cnt = {}
        self.sems = {}
        self.lastwait = {e: {} for e in ENGINES}
        for s_ in ("sx", "sxr", "so", "sw", "sp", "st", "ss", "sv", "sg"):
            h = es.enter_context(nc.semaphore(s_))
            self.sems[s_] = h
            self.cnt[s_] = 0

    def c(self, name):
        return self.cnt[name]

    def op(self, eng, fn, waits=(), inc=None, n=1):
        wl = [(s, v) for (s, v) in waits if v > 0]
        self.q[eng].append((wl, fn, inc, n))
        if inc is not None:
            self.cnt[inc] += n
        return self.cnt[inc] if inc is not None else None

    def wait(self, eng, *waits):
        self.op(eng, None, waits=waits)

    def flush(self):
        if not any(self.q[e] for e in ENGINES):
            return
        with self.nc.Block() as blk:
            for eng in ENGINES:
                ops = self.q[eng]
                if not ops:
                    continue
                sec = getattr(blk, eng)

                def body(e, ops=ops, eng=eng):
                    lw = self.lastwait[eng]
                    for (wl, fn, inc, n) in ops:
                        for (sname, v) in wl:
                            if v > lw.get(sname, -1):
                                e.wait_ge(self.sems[sname], v)
                                lw[sname] = v
                        if fn is not None:
                            instr = fn(e)
                            if inc is not None:
                                instr.then_inc(self.sems[inc], n)

                sec(body)
                self.q[eng] = []


def build_nc(debug=False):
    nc = bass.Bass("TRN2", target_bir_lowering=False, detect_race_conditions=False)

    x = nc.dram_tensor("x", [BPC, S, D], F32, kind="ExternalInput")
    wq = nc.dram_tensor("wq", [D, D], F32, kind="ExternalInput")
    wk = nc.dram_tensor("wk", [D, D], F32, kind="ExternalInput")
    wv = nc.dram_tensor("wv", [D, D], F32, kind="ExternalInput")
    wo = nc.dram_tensor("wo", [D, D], F32, kind="ExternalInput")
    wff = nc.dram_tensor("wff", [D, D], F32, kind="ExternalInput")
    bias = nc.dram_tensor("bias", [D], F32, kind="ExternalInput")
    out = nc.dram_tensor("out", [BPC, S, D], F32, kind="ExternalOutput")
    wdTs = nc.dram_tensor("wdTs", [128, 8 * D], BF16, kind="Internal")
    dbg = {}
    if debug:
        for nm, shp, dt in (("d_h1f", [128, 8 * S], BF16), ("d_qT", [128, 8 * S], BF16),
                            ("d_kT", [128, 8 * S], BF16), ("d_vaug", [128, NT * 1040], BF16),
                            ("d_onorm", [128, 8 * 512], BF16), ("d_r1", [128, 4 * D], F32),
                            ("d_xd", [128, D], BF16), ("d_wdT", [128, 8 * D], BF16),
                            ("d_e", [128, 4 * 1024], BF16),
                            ("d_ps", [128, 1024], F32), ("d_xdT", [128, D], BF16)):
            dbg[nm] = nc.dram_tensor(nm, shp, dt, kind="ExternalOutput")

    def x_tile_ap(lb, row0):
        return bass.AP(x, lb * S * D + row0 * D, [[D, 128], [1, D]])

    def x_block_ap(lb, t0, ntiles):
        return bass.AP(x, lb * S * D + t0 * 128 * D,
                       [[D, 128], [128 * D, ntiles], [1, D]])

    def out_tile_ap(lb, row0):
        return bass.AP(out, lb * S * D + row0 * D, [[D, 128], [1, D]])

    es = ExitStack()
    with es:
        E = Em(nc, es)
        sb = lambda name, shape, dt: es.enter_context(nc.sbuf_tensor(name, shape, dt))
        pst = lambda name, shape, dt: es.enter_context(nc.psum_tensor(name, shape, dt))

        qT = sb("qT", [128, 8 * S], BF16)
        kT = sb("kT", [128, 8 * S], BF16)
        v_aug = sb("v_aug", [128, NT * 1040], BF16)
        biasT = sb("biasT", [1, D], BF16)
        cst = sb("cst", [128, 8], F32)
        rowi = sb("rowi", [128, 128], BF16)
        coli = sb("coli", [128, 128], BF16)
        identb = sb("identb", [128, 128], BF16)
        trib = sb("trib", [128, 128], BF16)
        ones1 = sb("ones1", [1, 128], BF16)
        indA = sb("indA", [1, 128], BF16)
        indB = sb("indB", [1, 128], BF16)
        rbAB = sb("rbAB", [1, 1024], BF16)
        statsP = [sb(f"stats{p}", [128, 12], F32) for p in range(4)]

        psS = pst("psS", [128, 2048], F32)
        accAB = pst("accAB", [128, 1024], F32)
        pAux = pst("pAux", [128, 1024], F32)
        ptxv = pAux[:, :].bitcast(BF16)

        pstx_free = [("sv", 0), ("sv", 0)]

        E.op("gpsimd", lambda e: e.iota(rowi[:, :], [[0, 128]], channel_multiplier=1,
                                        allow_small_or_imprecise_dtypes=True))
        E.op("gpsimd", lambda e: e.iota(coli[:, :], [[1, 128]], channel_multiplier=0,
                                        allow_small_or_imprecise_dtypes=True))
        E.op("gpsimd", lambda e: e.memset(cst[:, C_EPS:C_EPS + 1], 1e-5))
        E.op("gpsimd", lambda e: e.memset(cst[:, C_M05:C_M05 + 1], -0.5))
        E.op("gpsimd", lambda e: e.memset(cst[:, C_ESC:C_ESC + 1], 0.125))
        E.op("gpsimd", lambda e: e.memset(cst[:, C_EBI:C_EBI + 1], -12.0))
        E.op("gpsimd", lambda e: e.memset(cst[:, C_M1:C_M1 + 1], -1.0))
        E.op("gpsimd", lambda e: e.memset(ones1[:, :], 1.0))
        E.op("gpsimd", lambda e: e.memset(indA[:, 0:64], 1.0))
        E.op("gpsimd", lambda e: e.memset(indA[:, 64:128], 0.0))
        E.op("gpsimd", lambda e: e.memset(indB[:, 0:64], 0.0))
        E.op("gpsimd", lambda e: e.memset(indB[:, 64:128], 1.0))
        ones_ap = bass.AP(v_aug, 64, [[NT * 1040, 128], [1040, NT], [65, H], [1, 1]])
        E.op("gpsimd", lambda e: e.memset(ones_ap, 1.0), inc="sg")
        sg_init = E.c("sg")

        E.op("vector", lambda e: e.tensor_tensor(identb[:, :], rowi[:, :], coli[:, :],
                                                 ALU.is_equal),
             waits=[("sg", sg_init)])
        E.op("vector", lambda e: e.tensor_tensor(trib[:, :], rowi[:, :], coli[:, :],
                                                 ALU.is_le), inc="sv")
        sv_consts = E.c("sv")

        E.op("gpsimd",
             lambda e: e.dma_start(bass.AP(biasT, 0, [[D, 1], [1, D]]),
                                   bass.AP(bias, 0, [[D, 1], [1, D]])),
             inc="sw", n=16)
        w_bias = E.c("sw")

        def ln_chain(par, src_ap, prev_w, sq_dump, sq_waits=(), sq_src=None):
            st_ = statsP[par]
            E.op("vector", lambda e, st_=st_, src_ap=src_ap: e.reduce_sum(
                st_[:, SUMX:SUMX + 1], src_ap, axis=AX),
                waits=[prev_w], inc="sv")
            v_sumx = E.c("sv")
            sq_src = src_ap if sq_src is None else sq_src
            E.op("scalar", lambda e, st_=st_, sq_src=sq_src, sq_dump=sq_dump: e.activation(
                sq_dump, sq_src, AF.Square, accum_out=st_[:, SUMSQ:SUMSQ + 1]),
                waits=[prev_w] + list(sq_waits), inc="ss")
            s_acc = E.c("ss")
            E.op("gpsimd", lambda e, st_=st_: e.tensor_scalar(
                st_[:, NEGMEAN:NEGMEAN + 1], st_[:, SUMX:SUMX + 1],
                -1.0 / D, None, ALU.mult),
                waits=[("ss", s_acc), ("sv", v_sumx)], inc="sg")
            E.wait("gpsimd", ("sg", E.c("sg")))
            E.op("gpsimd", lambda e, st_=st_: e.tensor_scalar(
                st_[:, SS_:SS_ + 1], st_[:, SUMSQ:SUMSQ + 1], 1.0 / D, None, ALU.mult),
                inc="sg")
            E.wait("gpsimd", ("sg", E.c("sg")))
            E.op("gpsimd", lambda e, st_=st_: e.tensor_tensor(
                st_[:, M2:M2 + 1], st_[:, NEGMEAN:NEGMEAN + 1],
                st_[:, NEGMEAN:NEGMEAN + 1], ALU.mult), inc="sg")
            E.wait("gpsimd", ("sg", E.c("sg")))
            E.op("gpsimd", lambda e, st_=st_: e.tensor_tensor(
                st_[:, VAR:VAR + 1], st_[:, SS_:SS_ + 1], st_[:, M2:M2 + 1],
                ALU.subtract), inc="sg")
            v_var = E.c("sg")
            E.op("scalar", lambda e, st_=st_: e.activation(
                st_[:, LNV:LNV + 1], st_[:, VAR:VAR + 1], AF.Ln,
                bias=cst[:, C_EPS:C_EPS + 1]),
                waits=[("sg", v_var)], inc="ss")
            E.wait("scalar", ("ss", E.c("ss")))
            E.op("scalar", lambda e, st_=st_: e.activation(
                st_[:, RSTD:RSTD + 1], st_[:, LNV:LNV + 1], AF.Exp,
                scale=cst[:, C_M05:C_M05 + 1]), inc="ss")
            E.wait("scalar", ("ss", E.c("ss")))
            E.op("scalar", lambda e, st_=st_: e.activation(
                st_[:, NEGMRS:NEGMRS + 1], st_[:, NEGMEAN:NEGMEAN + 1], AF.Identity,
                scale=st_[:, RSTD:RSTD + 1]), inc="ss")
            return E.c("ss")

        p34_done_st = 0
        p34_done_ss = 0
        stats_free = [("sv", 0)] * 4
        wd_spill_sw = 0
        total_out_dmas = 0

        for lb in range(BPC):
            with ExitStack() as es12:
                sb12 = lambda name, shape, dt: es12.enter_context(
                    nc.sbuf_tensor(name + f"_b{lb}", shape, dt))
                Wp = [sb12(f"Wp{p}", [128, 8 * D], BF16) for p in range(2)]
                h1f = sb12("h1f", [128, 8 * S], BF16)
                xin = [sb12(f"xin{p}", [128, D], F32) for p in range(4)]
                h1t = [sb12(f"h1t{p}", [128, D], BF16) for p in range(2)]
                wst = sb12("wst", [128, 512], F32)
                wab = sb12("wab", [128, 512], F32)
                wya = sb12("wya", [128, 512], F32)
                wm01 = sb12("wm01", [128, 512], F32)
                wq8 = sb12("wq8", [128, 512], FP8)
                wqd = [sb12(f"wqd{p}", [128, 512], BF16) for p in range(2)]
                wsgn = sb12("wsgn", [128, 512], BF16)
                wsc = sb12("wsc", [128, 16], F32)
                wrsc = sb12("wrsc", [128, 16], F32)
                wtst = sb12("wtst", [128, 512], BF16)

                def w_dma(src, dst, wait_st=0):
                    E.op("gpsimd",
                         lambda e, src=src, dst=dst: e.dma_start(
                             dst[:, :].rearrange("p (c m) -> p c m", m=D),
                             src.rearrange("(c p) m -> p c m", p=128)),
                         waits=[("st", wait_st)], inc="sw", n=16)
                    return E.c("sw")

                w_wq = w_dma(wq, Wp[0])
                w_wk = w_dma(wk, Wp[1])

                wd_state = {"sv": 0, "sabs": 0}
                wd_tr_st = [0, 0]
                wd_chain_sv = [0] * NT

                def wd_chain(sidx):
                    jc, kh = sidx // 2, sidx % 2
                    E.op("gpsimd", lambda e, jc=jc, kh=kh: e.dma_start(
                        wst[:, :],
                        bass.AP(wff, jc * 128 * D + kh * 512, [[D, 128], [1, 512]])),
                        waits=[("ss", wd_state["sabs"])], inc="sw", n=16)
                    ld = E.c("sw")
                    E.op("scalar", lambda e: e.activation(wab[:, :], wst[:, :], AF.Abs),
                         waits=[("sw", ld), ("sv", wd_state["sv"])])
                    E.op("scalar", lambda e: e.activation(wsgn[:, :], wst[:, :], AF.Sign),
                         inc="ss")
                    wd_state["sabs"] = E.c("ss")
                    def vedge():
                        E.wait("vector", ("sv", E.c("sv")))
                    E.op("vector", lambda e: e.reduce_max(
                        wsc[:, :], wab[:, :].rearrange("p (b k) -> p b k", k=32), axis=AX),
                        waits=[("ss", wd_state["sabs"])],
                        inc="sv")
                    vedge()
                    E.op("vector", lambda e: e.tensor_scalar(
                        wsc[:, :], wsc[:, :], 1.0 / 7.5, 1e-30, ALU.mult, ALU.max),
                        inc="sv")
                    vedge()
                    E.op("vector", lambda e: e.reciprocal(wrsc[:, :], wsc[:, :]), inc="sv")
                    vedge()
                    rsc_b = bass.AP(wrsc, 0, [[16, 128], [1, 16], [0, 32]])
                    sc_b = bass.AP(wsc, 0, [[16, 128], [1, 16], [0, 32]])
                    E.op("vector", lambda e, rsc_b=rsc_b: e.tensor_tensor(
                        wya[:, :].rearrange("p (b k) -> p b k", k=32),
                        wab[:, :].rearrange("p (b k) -> p b k", k=32), rsc_b, ALU.mult))
                    E.op("vector", lambda e: e.tensor_single_scalar(
                        wm01[:, :], wya[:, :], 1.0, ALU.is_lt))
                    E.op("vector", lambda e: e.tensor_tensor(
                        wya[:, :], wya[:, :], wm01[:, :], ALU.add))
                    E.op("vector", lambda e: e.tensor_copy(wq8[:, :], wya[:, :]))
                    wq_ = wqd[sidx % 2]
                    E.op("vector", lambda e, wq_=wq_: e.tensor_tensor(
                        wq_[:, :], wq8[:, :], wm01[:, :], ALU.subtract),
                        waits=[("st", wd_tr_st[sidx % 2])])
                    E.op("vector", lambda e, wq_=wq_, sc_b=sc_b: e.tensor_tensor(
                        wq_[:, :].rearrange("p (b k) -> p b k", k=32),
                        wq_[:, :].rearrange("p (b k) -> p b k", k=32), sc_b, ALU.mult))
                    E.op("vector", lambda e, wq_=wq_: e.tensor_tensor(
                        wq_[:, :], wq_[:, :], wsgn[:, :], ALU.mult), inc="sv")
                    wd_state["sv"] = E.c("sv")
                    wd_chain_sv[sidx] = E.c("sv")

                def wd_tail(sidx):
                    jc, kh = sidx // 2, sidx % 2
                    half = sidx % 2
                    wq_ = wqd[sidx % 2]
                    for c in range(4):
                        E.op("tensor", lambda e, c=c, half=half, wq_=wq_: e.transpose(
                            ptxv[:, half * 1024 + c * 128: half * 1024 + (c + 1) * 128],
                            wq_[:, c * 128:(c + 1) * 128], identb[:, :]),
                            waits=([("sv", wd_chain_sv[sidx]), ("sv", sv_consts),
                                    pstx_free[half]] if c == 0 else []),
                            inc="st" if c == 3 else None)
                    t_tr = E.c("st")
                    wd_tr_st[sidx % 2] = t_tr
                    E.op("vector", lambda e, half=half: e.tensor_copy(
                        wtst[:, :], ptxv[:, half * 1024: half * 1024 + 512]),
                        waits=[("st", t_tr), ("sp", wd_spill_guard[sidx])], inc="sv")
                    pstx_free[half] = ("sv", E.c("sv"))
                    s_ev = E.c("sv")
                    dest = bass.AP(wdTs, 4 * kh * D + jc * 128,
                                   [[8 * D, 128], [D, 4], [1, 128]])
                    E.op("sync", lambda e, dest=dest: e.dma_start(
                        dest, wtst[:, :].rearrange("p (c j) -> p c j", j=128)),
                        waits=[("sv", s_ev)], inc="sp", n=16)
                    return E.c("sp")

                wd_spill_guard = [0] * 17

                xin_free_sv = [0, 0, 0, 0]
                h1t_free_st = [p34_done_st, p34_done_st]
                sxw = 0
                p1_last_ss = 0
                h1f_ev = [0] * NT
                evac_free = [("ss", 0), ("sv", 0), ("sv", 0), ("sv", 0)]

                ln_nmrs = [0] * NT

                def p1_ln(t):
                    par = t % 4
                    E.op("sync", lambda e, lb=lb, t=t, par=par: e.dma_start(
                        xin[par][:, :], x_tile_ap(lb, t * 128)),
                        waits=[("sg", xin_free_sv[par])], inc="sx", n=16)
                    sx_t = E.c("sx")
                    E.wait("scalar", ("sx", sx_t))
                    E.wait("vector", ("sx", sx_t))
                    sq_dump = bass.AP(h1f, t * 128, [[8 * S, 128], [S, 8], [1, 128]])
                    ln_nmrs[t] = ln_chain(
                        par, xin[par][:, :], stats_free[par], sq_dump,
                        sq_src=xin[par][:, :].rearrange("p (c s) -> p c s", s=128))

                p1_ln(0)
                p1_ln(1)
                for t in range(NT):
                    par = t % 4
                    hpar = t % 2
                    s_nmrs = ln_nmrs[t]
                    stp = statsP[par]
                    E.op("gpsimd", lambda e, par=par, hpar=hpar, stp=stp: e.tensor_scalar(
                        h1t[hpar][:, :], xin[par][:, :],
                        stp[:, RSTD:RSTD + 1], stp[:, NEGMRS:NEGMRS + 1],
                        ALU.mult, ALU.add),
                        waits=[("ss", s_nmrs), ("st", h1t_free_st[hpar])], inc="sg")
                    v_h1 = E.c("sg")
                    xin_free_sv[par] = v_h1
                    stats_free[par] = ("sg", v_h1)
                    if t + 2 < NT:
                        p1_ln(t + 2)
                    if t >= 8:
                        mo, sblk = (t - 8) // 4, ((t - 8) // 2) % 2
                        bank = mo * 2 + sblk
                        for mc in range(4 * (t % 2), 4 * (t % 2) + 4):
                            E.op("tensor", lambda e, mo=mo, mc=mc, sblk=sblk, bank=bank: e.matmul(
                                psS[:, bank * 512:(bank + 1) * 512],
                                Wp[0][:, mc * D + mo * 128: mc * D + (mo + 1) * 128],
                                h1f[:, mc * S + sblk * 512: mc * S + sblk * 512 + 512],
                                start=(mc == 0), stop=(mc == 7)),
                                waits=([("sw", w_wq),
                                        ("sv", h1f_ev[4 * sblk + 3])]
                                       if mc == 0 else []),
                                inc="st" if mc == 7 else None)
                    half = hpar
                    for c in range(8):
                        E.op("tensor", lambda e, c=c, half=half, hpar=hpar: e.transpose(
                            ptxv[:, half * 1024 + c * 128: half * 1024 + (c + 1) * 128],
                            h1t[hpar][:, c * 128:(c + 1) * 128], identb[:, :]),
                            waits=([("sg", v_h1), ("sv", sv_consts),
                                    pstx_free[half]] if c == 0 else []),
                            inc="st" if c == 7 else None)
                    t_tr = E.c("st")
                    h1t_free_st[hpar] = t_tr
                    dest = bass.AP(h1f, t * 128, [[8 * S, 128], [S, 8], [1, 128]])
                    E.op("vector", lambda e, dest=dest, half=half: e.tensor_copy(
                        dest, ptxv[:, half * 1024: half * 1024 + 1024].rearrange('p (c s) -> p c s', s=128)),
                        waits=[("st", t_tr)], inc="sv")
                    pstx_free[half] = ("sv", E.c("sv"))
                    p1_last_ss = E.c("sv")
                    h1f_ev[t] = E.c("sv")
                h1_ready = p1_last_ss
                t_mm01 = E.c("st")
                for mo in (0, 1):
                    for sblk in (0, 1):
                        bank = mo * 2 + sblk
                        E.op("vector", lambda e, mo=mo, sblk=sblk, bank=bank: e.tensor_copy(
                            qT[:, mo * S + sblk * 512: mo * S + sblk * 512 + 512],
                            psS[:, bank * 512:(bank + 1) * 512]),
                            waits=[("st", t_mm01)] if bank == 0 else [], inc="sv")
                        evac_free[bank] = ("sv", E.c("sv"))

                qT_done_st = 0

                for phase, (wslot, dstT, wsem) in enumerate(
                        ((0, qT, w_wq), (1, kT, w_wk))):
                    for mo in range(8):
                        sblks = ((2, 3) if mo < 2 else (0, 1, 2, 3)) \
                            if phase == 0 else (0, 1, 2, 3)
                        for sblk in sblks:
                            for mc in range(8):
                                E.op("tensor", lambda e, wslot=wslot, mo=mo, mc=mc, sblk=sblk: e.matmul(
                                    psS[:, sblk * 512:(sblk + 1) * 512],
                                    Wp[wslot][:, mc * D + mo * 128: mc * D + (mo + 1) * 128],
                                    h1f[:, mc * S + sblk * 512: mc * S + sblk * 512 + 512],
                                    start=(mc == 0), stop=(mc == 7)),
                                    waits=([("sw", wsem),
                                            ("sv", h1f_ev[4 * sblk + 3]),
                                            evac_free[sblk]]
                                           if mc == 0 else []),
                                    inc="st" if (mc == 7 and sblk == sblks[-1]) else None)
                        t_mm = E.c("st")
                        for sblk in sblks:
                            E.op("vector", lambda e, mo=mo, sblk=sblk, dstT=dstT: e.tensor_copy(
                                dstT[:, mo * S + sblk * 512: mo * S + sblk * 512 + 512],
                                psS[:, sblk * 512:(sblk + 1) * 512]),
                                waits=[("st", t_mm)] if sblk == sblks[0] else [], inc="sv")
                            evac_free[sblk] = ("sv", E.c("sv"))
                        if lb == 0:
                            k = phase * 8 + mo
                            wd_chain(k)
                            if k > 0:
                                wd_spill_guard[k] = wd_tail(k - 1)
                    if phase == 0:
                        qT_done_st = E.c("st")
                        w_wv = w_dma(wv, Wp[0], wait_st=qT_done_st)

                for t in range(NT):
                    pp = t % 2
                    pofs = pp * 1024
                    for mc in range(8):
                        for nb in range(2):
                            E.op("tensor", lambda e, mc=mc, nb=nb, t=t, pofs=pofs: e.matmul(
                                psS[:, pofs + nb * 512: pofs + nb * 512 + 512],
                                h1f[:, mc * S + t * 128: mc * S + (t + 1) * 128],
                                Wp[0][:, mc * D + nb * 512: mc * D + nb * 512 + 512],
                                start=(mc == 0), stop=(mc == 7)),
                                waits=([("sw", w_wv), evac_free[2 * pp],
                                        evac_free[2 * pp + 1]]
                                       if (mc == 0 and nb == 0) else []),
                                inc="st" if (mc == 7 and nb == 1) else None)
                    t_mm = E.c("st")
                    dest = bass.AP(v_aug, t * 1040, [[NT * 1040, 128], [65, 16], [1, 64]])
                    E.op("vector", lambda e, dest=dest, pofs=pofs: e.tensor_copy(
                        dest, bass.AP(psS, pofs, [[2048, 128], [64, 16], [1, 64]])),
                        waits=[("st", t_mm), ("sg", sg_init)], inc="sv")
                    evac_free[2 * pp] = ("sv", E.c("sv"))
                    evac_free[2 * pp + 1] = ("sv", E.c("sv"))

                if debug and lb == 0:
                    E.op("sync", lambda e: e.dma_start(dbg["d_h1f"][:, :], h1f[:, :]),
                         waits=[("sv", h1_ready)], inc="sp", n=16)
                    E.op("sync", lambda e: e.dma_start(dbg["d_qT"][:, :], qT[:, :]),
                         waits=[("sv", E.c("sv"))], inc="sp", n=16)
                    E.op("sync", lambda e: e.dma_start(dbg["d_kT"][:, :], kT[:, :]),
                         waits=[("sv", E.c("sv"))], inc="sp", n=16)
                    E.op("sync", lambda e: e.dma_start(dbg["d_vaug"][:, :], v_aug[:, :]),
                         waits=[("sv", E.c("sv"))], inc="sp", n=16)
                    E.wait("sync", ("sp", E.c("sp")))
                if lb == 0:
                    wd_spill_sw = wd_tail(NT - 1)
                p2_done_st = E.c("st")
                p2_done_ss = E.c("ss")
                p2_done_sv = E.c("sv")
                E.flush()

            with ExitStack() as es34:
                sb34 = lambda name, shape, dt: es34.enter_context(
                    nc.sbuf_tensor(name + f"_b{lb}", shape, dt))
                Wo_sb = sb34("Wo_sb", [128, 8 * D], BF16)
                wdT = sb34("wdT", [128, 8 * D], BF16)
                e_sb = sb34("e_sb", [128, 4 * 1024], BF16)
                out_norm = sb34("out_norm", [128, 8 * 512], BF16)
                xr = sb34("xr", [128, 4 * D], F32)
                r2buf = sb34("r2buf", [128, 2 * D], F32)
                R_sb = sb34("R_sb", [128, 1024], BF16)
                rbAB_b = sb34("rbAB_b", [1, 1024], BF16)
                ah = sb34("ah", [128, D], F32)
                m01 = sb34("m01", [128, D], BF16)
                q8t = sb34("q8t", [128, D], FP8)
                sgn = sb34("sgn", [128, D], BF16)
                qa = sb34("qa", [128, D], BF16)
                xd_blk = sb34("xd_blk", [128, 4 * D], BF16)
                xdT = [sb34(f"xdT{p}", [128, D], BF16) for p in range(2)]
                bmax = sb34("bmax", [128, 32], F32)
                scl = sb34("scl", [128, 32], F32)
                rscl = sb34("rscl", [128, 32], F32)

                E.op("gpsimd", lambda e: e.dma_start(
                    Wo_sb[:, :].rearrange("p (c m) -> p c m", m=D),
                    wo.rearrange("(c p) m -> p c m", p=128)),
                    inc="sw", n=16)
                w_wo = E.c("sw")
                E.op("gpsimd", lambda e: e.dma_start(wdT[:, :], wdTs[:, :]),
                     waits=[("sp", wd_spill_sw)], inc="sw", n=16)
                w_wdT = E.c("sw")

                slot_free_st = [p2_done_st] * 4
                pair_half_free = [p2_done_ss, p2_done_ss]
                acc_free2 = [0, 0]
                ps_guard_sv = 0
                r2_free_so = [0, 0]
                out_norm_free_st = p2_done_st
                R_guard_sv2 = [0, 0]
                quant_buf_free_sv = 0
                xd_free_ss = p2_done_ss
                xdT_free_st = [p2_done_st, p2_done_st]
                rb_free_P = [0, 0]

                E.op("sync", lambda e, lb=lb: e.dma_start(
                    xr[:, :].rearrange("p (t m) -> p t m", m=D), x_block_ap(lb, 0, 4)),
                    inc="sxr", n=16)
                xr_loaded = [E.c("sxr")] * 4
                xr_read_ss = [0, 0, 0, 0]

                psW_free_sv = [0, 0]

                ffn_vxd = [0, 0, 0, 0]
                r1_ready_sv = [0, 0, 0, 0]
                xdblk_free_st = p2_done_st
                norm_hold = [0]

                def emit_wo_resid(i):
                    nonlocal out_norm_free_st, ps_guard_sv
                    norm_done_sv = norm_hold[0]
                    for qs in range(4):
                        pp = qs % 2
                        pofs = pp * 1024
                        for mc in range(8):
                            for nb in range(2):
                                E.op("tensor", lambda e, mc=mc, nb=nb, qs=qs, pofs=pofs: e.matmul(
                                    psS[:, pofs + nb * 512: pofs + nb * 512 + 512],
                                    out_norm[:, mc * 512 + qs * 128: mc * 512 + (qs + 1) * 128],
                                    Wo_sb[:, mc * D + nb * 512: mc * D + nb * 512 + 512],
                                    start=(mc == 0), stop=(mc == 7)),
                                    waits=([("sv", norm_done_sv), ("sw", w_wo),
                                            ("sv", psW_free_sv[pp]),
                                            ("ss", pair_half_free[0]),
                                            ("ss", pair_half_free[1])]
                                           if (mc == 0 and nb == 0) else []),
                                    inc="st" if (mc == 7 and nb == 1) else None)
                        t_wo = E.c("st")
                        E.op("vector", lambda e, qs=qs, pofs=pofs: e.tensor_tensor(
                            xr[:, qs * D:(qs + 1) * D], xr[:, qs * D:(qs + 1) * D],
                            psS[:, pofs:pofs + 1024], ALU.add),
                            waits=[("st", t_wo), ("sxr", xr_loaded[qs])], inc="sv")
                        psW_free_sv[pp] = E.c("sv")
                        r1_ready_sv[qs] = E.c("sv")
                    out_norm_free_st = E.c("st")
                    ps_guard_sv = max(psW_free_sv)
                    if debug and lb == 0 and i == 0:
                        E.op("sync", lambda e: e.dma_start(dbg["d_r1"][:, :], xr[:, :]),
                             waits=[("sv", r1_ready_sv[3])], inc="sp", n=16)
                        E.wait("sync", ("sp", E.c("sp")))

                def emit_pre_a(i, qs):
                    xr_t = xr[:, qs * D:(qs + 1) * D]
                    st_ = statsP[qs]
                    E.op("vector", lambda e, st_=st_, xr_t=xr_t: e.reduce_sum(
                        st_[:, SUMX:SUMX + 1], xr_t, axis=AX),
                        waits=[stats_free[qs]], inc="sv")
                    E.op("gpsimd", lambda e, xr_t=xr_t: e.tensor_tensor(
                        ah[:, :], xr_t, xr_t, ALU.mult),
                        waits=[("sv", quant_buf_free_sv),
                               ("sv", r1_ready_sv[qs])], inc="sg")
                    g_sq = E.c("sg")
                    E.op("vector", lambda e, st_=st_: e.reduce_sum(
                        st_[:, SUMSQ:SUMSQ + 1], ah[:, :], axis=AX),
                        waits=[("sg", g_sq)], inc="sv")
                    v_ssq = E.c("sv")
                    E.op("gpsimd", lambda e, st_=st_: e.tensor_scalar(
                        st_[:, NEGMEAN:NEGMEAN + 1], st_[:, SUMX:SUMX + 1],
                        -1.0 / D, None, ALU.mult),
                        waits=[("sv", v_ssq)], inc="sg")
                    E.wait("gpsimd", ("sg", E.c("sg")))
                    E.op("gpsimd", lambda e, st_=st_: e.tensor_scalar(
                        st_[:, SS_:SS_ + 1], st_[:, SUMSQ:SUMSQ + 1],
                        1.0 / D, None, ALU.mult), inc="sg")
                    E.wait("gpsimd", ("sg", E.c("sg")))
                    E.op("gpsimd", lambda e, st_=st_: e.tensor_tensor(
                        st_[:, M2:M2 + 1], st_[:, NEGMEAN:NEGMEAN + 1],
                        st_[:, NEGMEAN:NEGMEAN + 1], ALU.mult), inc="sg")
                    E.wait("gpsimd", ("sg", E.c("sg")))
                    E.op("gpsimd", lambda e, st_=st_: e.tensor_tensor(
                        st_[:, VAR:VAR + 1], st_[:, SS_:SS_ + 1], st_[:, M2:M2 + 1],
                        ALU.subtract), inc="sg")
                    v_var = E.c("sg")
                    E.op("scalar", lambda e, st_=st_: e.activation(
                        st_[:, LNV:LNV + 1], st_[:, VAR:VAR + 1], AF.Ln,
                        bias=cst[:, C_EPS:C_EPS + 1]),
                        waits=[("sg", v_var)])
                    E.op("scalar", lambda e, st_=st_: e.activation(
                        st_[:, RSTD:RSTD + 1], st_[:, LNV:LNV + 1], AF.Exp,
                        scale=cst[:, C_M05:C_M05 + 1]), inc="ss")
                    return (E.c("ss"), v_ssq)

                def emit_pre_b(i, qs, ab_):
                    s_rstd, v_ssq = ab_
                    xr_t = xr[:, qs * D:(qs + 1) * D]
                    stp = statsP[qs]
                    E.op("gpsimd", lambda e, stp=stp: e.tensor_tensor(
                        stp[:, NEGMRS:NEGMRS + 1], stp[:, NEGMEAN:NEGMEAN + 1],
                        stp[:, RSTD:RSTD + 1], ALU.mult),
                        waits=[("ss", s_rstd)], inc="sg")
                    E.wait("gpsimd", ("sg", E.c("sg")))
                    E.op("gpsimd", lambda e, xr_t=xr_t, stp=stp: e.tensor_scalar(
                        ah[:, :], xr_t, stp[:, RSTD:RSTD + 1],
                        stp[:, NEGMRS:NEGMRS + 1], ALU.mult, ALU.add),
                        waits=[("sv", v_ssq)], inc="sg")
                    g_t = E.c("sg")
                    xr_read_ss[qs] = g_t
                    stats_free[qs] = ("sg", g_t)
                    E.op("vector", lambda e: e.tensor_single_scalar(
                        sgn[:, :], ah[:, :], 0.0, ALU.is_ge),
                        waits=[("sg", g_t)])
                    E.op("vector", lambda e: e.tensor_scalar(
                        sgn[:, :], sgn[:, :], 2.0, -1.0, ALU.mult, ALU.add))
                    E.op("vector", lambda e: e.tensor_tensor(
                        ah[:, :], ah[:, :], sgn[:, :], ALU.mult), inc="sv")
                    return E.c("sv")

                def emit_pre_c(i, qs, v_abs):
                    nonlocal quant_buf_free_sv
                    if True:
                        E.op("vector", lambda e: e.reduce_max(
                            bmax[:, :], ah[:, :].rearrange("p (b k) -> p b k", k=32),
                            axis=AX), inc="sv")
                        E.wait("vector", ("sv", E.c("sv")))
                        E.op("vector", lambda e: e.tensor_scalar(
                            scl[:, :], bmax[:, :], 1.0 / 7.5, 1e-30, ALU.mult, ALU.max),
                            inc="sv")
                        E.wait("vector", ("sv", E.c("sv")))
                        E.op("vector", lambda e: e.reciprocal(rscl[:, :], scl[:, :]),
                             inc="sv")
                        E.wait("vector", ("sv", E.c("sv")))
                        rsc_b = bass.AP(rscl, 0, [[32, 128], [1, 32], [0, 32]])
                        sc_b = bass.AP(scl, 0, [[32, 128], [1, 32], [0, 32]])
                        ah3 = ah[:, :].rearrange("p (b k) -> p b k", k=32)
                        E.op("vector", lambda e, rsc_b=rsc_b, ah3=ah3: e.tensor_tensor(
                            ah3, ah3, rsc_b, ALU.mult))
                        E.op("vector", lambda e: e.tensor_single_scalar(
                            m01[:, :], ah[:, :], 1.0, ALU.is_lt))
                        E.op("vector", lambda e: e.tensor_tensor(
                            ah[:, :], ah[:, :], m01[:, :], ALU.add))
                        E.op("vector", lambda e: e.tensor_copy(q8t[:, :], ah[:, :]))
                        E.op("vector", lambda e: e.tensor_tensor(
                            qa[:, :], q8t[:, :], m01[:, :], ALU.subtract))
                        qa3 = qa[:, :].rearrange("p (b k) -> p b k", k=32)
                        E.op("vector", lambda e, sc_b=sc_b, qa3=qa3: e.tensor_tensor(
                            qa3, qa3, sc_b, ALU.mult))
                        E.op("vector", lambda e, qs=qs: e.tensor_tensor(
                            xd_blk[:, qs * D:(qs + 1) * D], qa[:, :], sgn[:, :], ALU.mult),
                            waits=[("st", xdblk_free_st)], inc="sv")
                        ffn_vxd[qs] = E.c("sv")
                        quant_buf_free_sv = ffn_vxd[qs]
                        if debug and lb == 0 and i == 0 and qs == 0:
                            E.op("sync", lambda e: e.dma_start(
                                dbg["d_xd"][:, :], xd_blk[:, 0:D]),
                                waits=[("sv", ffn_vxd[0])], inc="sp", n=16)
                            E.wait("sync", ("sp", E.c("sp")))

                def emit_ffn_pre_qs(i, qs):
                    s_nmrs = emit_pre_a(i, qs)
                    s_ah = emit_pre_b(i, qs, s_nmrs)
                    emit_pre_c(i, qs, s_ah)

                def emit_ffn_post(i):
                    nonlocal xdblk_free_st, ps_guard_sv
                    for qs in range(4):
                        emit_ffn_post_qs(i, qs)
                    xdblk_free_st = E.c("st")
                    ps_guard_sv = max(psW_free_sv)

                def emit_ffn_post_qs(i, qs):
                    nonlocal total_out_dmas
                    if True:
                        pp2 = qs % 2
                        pofs = pp2 * 1024
                        for c in range(8):
                            E.op("tensor", lambda e, c=c, pp2=pp2, qs=qs: e.transpose(
                                ptxv[:, pp2 * 1024 + c * 128: pp2 * 1024 + (c + 1) * 128],
                                xd_blk[:, qs * D + c * 128: qs * D + (c + 1) * 128],
                                identb[:, :]),
                                waits=([("sv", ffn_vxd[qs]), pstx_free[pp2],
                                        ("sv", acc_free2[1])]
                                       if c == 0 else []),
                                inc="st" if c == 7 else None)
                        t_tx = E.c("st")
                        E.op("scalar", lambda e, pp2=pp2: e.copy(
                            xdT[pp2][:, :], ptxv[:, pp2 * 1024:(pp2 + 1) * 1024]),
                            waits=[("st", t_tx), ("st", xdT_free_st[pp2])], inc="ss")
                        s_xdT = E.c("ss")
                        pstx_free[pp2] = ("ss", s_xdT)
                        for nb in range(2):
                            E.op("tensor", lambda e, nb=nb, pofs=pofs: e.matmul(
                                psS[:, pofs + nb * 512: pofs + nb * 512 + 512],
                                ones1[0:1, :], biasT[0:1, nb * 512:(nb + 1) * 512],
                                start=True, stop=False),
                                waits=([("ss", s_xdT), ("sv", psW_free_sv[pp2]),
                                        ("sw", w_bias), ("sw", w_wdT), ("sg", sg_init),
                                        ("ss", pair_half_free[0]), ("ss", pair_half_free[1]),
                                        ("sv", R_guard_sv2[0]), ("sv", R_guard_sv2[1])]
                                       if nb == 0 else []))
                        for kc in range(8):
                            for nb in range(2):
                                E.op("tensor", lambda e, kc=kc, nb=nb, pp2=pp2, pofs=pofs: e.matmul(
                                    psS[:, pofs + nb * 512: pofs + nb * 512 + 512],
                                    xdT[pp2][:, kc * 128:(kc + 1) * 128],
                                    wdT[:, kc * D + nb * 512: kc * D + nb * 512 + 512],
                                    start=False, stop=(kc == 7)),
                                    inc="st" if (kc == 7 and nb == 1) else None)
                        t_ffn = E.c("st")
                        xdT_free_st[pp2] = t_ffn
                        if debug and lb == 0 and i == 0 and qs == 3:
                            E.op("scalar", lambda e, pofs=pofs: e.copy(
                                ah[:, :], psS[:, pofs:pofs + 1024]),
                                waits=[("st", t_ffn)], inc="ss")
                            s_dbg = E.c("ss")
                            E.op("sync", lambda e: e.dma_start(dbg["d_ps"][:, :], ah[:, :]),
                                 waits=[("ss", s_dbg)], inc="sp", n=16)
                            E.op("sync", lambda e, pp2=pp2: e.dma_start(
                                dbg["d_xdT"][:, :], xdT[pp2][:, :]),
                                waits=[("ss", s_dbg)], inc="sp", n=16)
                            E.wait("sync", ("sp", E.c("sp")))
                        E.op("vector", lambda e, qs=qs, pp2=pp2, pofs=pofs: e.tensor_tensor(
                            r2buf[:, pp2 * D:(pp2 + 1) * D], xr[:, qs * D:(qs + 1) * D],
                            psS[:, pofs:pofs + 1024], ALU.add),
                            waits=[("st", t_ffn), ("so", r2_free_so[pp2])], inc="sv")
                        v_r2 = E.c("sv")
                        psW_free_sv[pp2] = v_r2
                        E.op("sync", lambda e, lb=lb, i=i, qs=qs, pp2=pp2: e.dma_start(
                            out_tile_ap(lb, (i * 4 + qs) * 128),
                            r2buf[:, pp2 * D:(pp2 + 1) * D]),
                            waits=[("sv", v_r2)], inc="so", n=16)
                        r2_free_so[pp2] = E.c("so")
                        total_out_dmas += 1
                        if i + 1 < NI:
                            E.op("sync", lambda e, lb=lb, i=i, qs=qs: e.dma_start(
                                xr[:, qs * D:(qs + 1) * D],
                                x_tile_ap(lb, ((i + 1) * 4 + qs) * 128)),
                                waits=[("sv", v_r2), ("sg", xr_read_ss[qs])],
                                inc="sxr", n=16)
                            xr_loaded[qs] = E.c("sxr")

                pend_pre = [None]

                def att_hpg(i, hpg, prime):
                    njt = 4 * i + 4
                    nfull = 4 * i
                    if True:
                        exp_of = [{}, {}]

                        def emit_score2(P, jt):
                            hp = 2 * hpg + P
                            d = jt - nfull
                            full = d < 0
                            Nd = 512 if full else 512 - 128 * d
                            qo = i * 512 + (0 if full else 128 * d)
                            for p0, base in ((0, 0), (64, 1024)):
                                E.op("tensor", lambda e, p0=p0, base=base, P=P, hp=hp, jt=jt, qo=qo, Nd=Nd: e.matmul(
                                    psS[:, base + P * 512: base + P * 512 + Nd],
                                    kT[p0:p0 + 64, hp * S + jt * 128: hp * S + (jt + 1) * 128],
                                    qT[p0:p0 + 64, hp * S + qo: hp * S + qo + Nd],
                                    start=True, stop=True),
                                    waits=[("ss", pair_half_free[P]), ("sv", ps_guard_sv),
                                           ("sv", R_guard_sv2[P])],
                                    inc="st" if base == 1024 else None)
                            return E.c("st")

                        def emit_exp2(P, jt, t_sc):
                            d = jt - nfull
                            full = d < 0
                            Nd = 512 if full else 512 - 128 * d
                            slot = P * 2 + jt % 2
                            src = bass.AP(psS, P * 512, [[2048, 128], [1024, 2], [1, Nd]])
                            dst = bass.AP(e_sb, slot * 1024, [[4096, 128], [512, 2], [1, Nd]])
                            E.op("scalar", lambda e, src=src, dst=dst: e.activation(
                                dst, src, AF.Exp, bias=cst[:, C_EBI:C_EBI + 1],
                                scale=cst[:, C_ESC:C_ESC + 1]),
                                waits=[("st", t_sc), ("st", slot_free_st[slot]),
                                       ("sg", sg_init)],
                                inc="ss")
                            sdone = E.c("ss")
                            pair_half_free[P] = sdone
                            mask_v = 0
                            if not full:
                                mdst = bass.AP(e_sb, slot * 1024,
                                               [[4096, 128], [512, 2], [1, 128]])
                                msrc = bass.AP(trib, 0, [[128, 128], [0, 2], [1, 128]])
                                E.op("vector", lambda e, mdst=mdst, msrc=msrc: e.tensor_tensor(
                                    mdst, mdst, msrc, ALU.mult),
                                    waits=[("ss", sdone)], inc="sv")
                                mask_v = E.c("sv")
                            exp_of[P][jt] = (sdone, mask_v)

                        def emit_av2(P, jt):
                            hp = 2 * hpg + P
                            d = jt - nfull
                            full = d < 0
                            Nd = 512 if full else 512 - 128 * d
                            col0 = 0 if full else 128 * d
                            slot = P * 2 + jt % 2
                            acc = accAB if P == 0 else pAux
                            sdone, mask_v = exp_of[P][jt]
                            for aofs, hofs in ((0, 0), (512, 512)):
                                hh = 2 * hp + (0 if aofs == 0 else 1)
                                extra = ([pstx_free[0], pstx_free[1]]
                                         if (P == 1 and jt == 0) else [])
                                E.op("tensor", lambda e, jt=jt, hh=hh, acc=acc, aofs=aofs, col0=col0, Nd=Nd, slot=slot, hofs=hofs, njt=njt: e.matmul(
                                    acc[0:65, aofs + col0: aofs + col0 + Nd],
                                    v_aug[:, jt * 1040 + hh * 65: jt * 1040 + hh * 65 + 65],
                                    e_sb[:, slot * 1024 + hofs: slot * 1024 + hofs + Nd],
                                    start=(jt == 0), stop=(jt == njt - 1)),
                                    waits=[("ss", sdone), ("sv", mask_v),
                                           ("sv", acc_free2[P])] + extra,
                                    inc="st" if aofs == 512 else None)
                            slot_free_st[slot] = E.c("st")

                        if prime.get("key") == (i, hpg):
                            exp_of[0][0] = prime["e0"]
                            exp_of[1][0] = prime["e1"]
                            prime["key"] = None
                        else:
                            t_sc0 = emit_score2(0, 0)
                            emit_exp2(0, 0, t_sc0)
                            t_sc1 = emit_score2(1, 0)
                            emit_exp2(1, 0, t_sc1)
                        chst = {}
                        for jt in range(njt):
                            for P in range(2):
                                if jt + 1 < njt:
                                    t_sc = emit_score2(P, jt + 1)
                                    emit_exp2(P, jt + 1, t_sc)
                                emit_av2(P, jt)
                            if pend_pre[0] is not None:
                                if jt == 1:
                                    chst["nmrs"] = emit_pre_a(pend_pre[0], hpg)
                                elif jt == 3:
                                    chst["ah"] = emit_pre_b(
                                        pend_pre[0], hpg, chst["nmrs"])
                                elif nfull >= 8 and jt == nfull - 4:
                                    emit_pre_c(pend_pre[0], hpg, chst["ah"])
                                    chst["c_done"] = True
                        av_done_st = E.c("st")

                        for P in range(2):
                            hp = 2 * hpg + P
                            acc = accAB if P == 0 else pAux
                            lnrow = (accAB if P == 0 else pAux)[96:97, 0:1024]
                            rbP = rbAB if P == 0 else rbAB_b
                            E.op("scalar", lambda e, acc=acc, lnrow=lnrow: e.activation(
                                lnrow, acc[64:65, 0:1024], AF.Ln),
                                waits=[("st", av_done_st)])
                            E.op("scalar", lambda e, lnrow=lnrow, rbP=rbP: e.activation(
                                rbP[0:1, 0:1024], lnrow, AF.Exp,
                                scale=cst[0:1, C_M1:C_M1 + 1]),
                                waits=[("st", rb_free_P[P])], inc="ss")
                            s_rb = E.c("ss")
                            E.op("tensor", lambda e, P=P, rbP=rbP: e.matmul(
                                psS[:, P * 512:(P + 1) * 512], indA[0:1, :],
                                rbP[0:1, 0:512],
                                start=True, stop=False),
                                waits=[("ss", s_rb), ("ss", pair_half_free[P]),
                                       ("sg", sg_init)])
                            E.op("tensor", lambda e, P=P, rbP=rbP: e.matmul(
                                psS[:, P * 512:(P + 1) * 512], indB[0:1, :],
                                rbP[0:1, 512:1024],
                                start=False, stop=True), inc="st")
                            t_R = E.c("st")
                            rb_free_P[P] = t_R
                            E.op("vector", lambda e, P=P: e.tensor_copy(
                                R_sb[:, P * 512:(P + 1) * 512],
                                psS[:, P * 512:(P + 1) * 512]),
                                waits=[("st", t_R), ("sv", acc_free2[P])], inc="sv")
                            s_R = E.c("sv")
                            R_guard_sv2[P] = s_R
                            E.op("vector", lambda e, hp=hp, P=P, acc=acc: e.tensor_tensor(
                                out_norm[0:64, hp * 512:(hp + 1) * 512],
                                acc[0:64, 0:512], R_sb[0:64, P * 512:(P + 1) * 512],
                                ALU.mult),
                                waits=[("sv", s_R), ("st", out_norm_free_st)])
                            E.op("vector", lambda e, hp=hp, P=P, acc=acc: e.tensor_tensor(
                                out_norm[64:128, hp * 512:(hp + 1) * 512],
                                acc[0:64, 512:1024],
                                R_sb[64:128, P * 512:(P + 1) * 512], ALU.mult),
                                inc="sv")
                            acc_free2[P] = E.c("sv")
                        if pend_pre[0] is not None and not chst.get("c_done"):
                            emit_pre_c(pend_pre[0], hpg, chst["ah"])

                def prime_next(i2, prime):
                    for P in range(2):
                        hp = P
                        for p0, base in ((0, 0), (64, 1024)):
                            E.op("tensor", lambda e, p0=p0, base=base, P=P, hp=hp, i2=i2: e.matmul(
                                psS[:, base + P * 512: base + P * 512 + 512],
                                kT[p0:p0 + 64, hp * S: hp * S + 128],
                                qT[p0:p0 + 64, hp * S + i2 * 512: hp * S + i2 * 512 + 512],
                                start=True, stop=True),
                                waits=[("ss", pair_half_free[P]), ("sv", ps_guard_sv),
                                       ("sv", R_guard_sv2[P])],
                                inc="st" if base == 1024 else None)
                        t_sc = E.c("st")
                        slot = P * 2
                        src = bass.AP(psS, P * 512, [[2048, 128], [1024, 2], [1, 512]])
                        dst = bass.AP(e_sb, slot * 1024, [[4096, 128], [512, 2], [1, 512]])
                        E.op("scalar", lambda e, src=src, dst=dst: e.activation(
                            dst, src, AF.Exp, bias=cst[:, C_EBI:C_EBI + 1],
                            scale=cst[:, C_ESC:C_ESC + 1]),
                            waits=[("st", t_sc), ("st", slot_free_st[slot]),
                                   ("sg", sg_init)],
                            inc="ss")
                        pair_half_free[P] = E.c("ss")
                        prime["e" + str(P)] = (E.c("ss"), 0)
                    prime["key"] = (i2, 0)

                prime = {}
                for i in range(NI):
                    for hpg in range(4):
                        att_hpg(i, hpg, prime)
                    norm_hold[0] = E.c("sv")
                    if debug and lb == 0 and i == 0:
                        E.op("sync", lambda e: e.dma_start(dbg["d_onorm"][:, :], out_norm[:, :]),
                             waits=[("sv", norm_hold[0])], inc="sp", n=16)
                        E.op("sync", lambda e: e.dma_start(dbg["d_wdT"][:, :], wdT[:, :]),
                             waits=[("sw", w_wdT)], inc="sp", n=16)
                        E.wait("sync", ("sp", E.c("sp")))

                    if DEFER_FFN and i + 1 < NI:
                        prime_next(i + 1, prime)
                    if DEFER_FFN and i > 0:
                        emit_ffn_post(i - 1)
                    emit_wo_resid(i)
                    if DEFER_FFN:
                        pend_pre[0] = i
                    else:
                        for qs in range(4):
                            emit_ffn_pre_qs(i, qs)
                        emit_ffn_post(i)

                if DEFER_FFN:
                    for qs in range(4):
                        emit_ffn_pre_qs(NI - 1, qs)
                        emit_ffn_post_qs(NI - 1, qs)
                p34_done_st = E.c("st")
                p34_done_ss = E.c("ss")
                E.flush()

        E.wait("sync", ("so", E.c("so")))
        E.flush()

    return nc


_NC_CACHE = None


def kernel(L0_attn_x, L0_attn_Wq, L0_attn_Wk, L0_attn_Wv, L0_attn_Wo,
           L0_ffn_weight, L0_ffn_bias):
    global _NC_CACHE
    xh = np.ascontiguousarray(np.asarray(L0_attn_x, dtype=np.float32))
    ins = {
        "wq": np.ascontiguousarray(np.asarray(L0_attn_Wq, dtype=np.float32)),
        "wk": np.ascontiguousarray(np.asarray(L0_attn_Wk, dtype=np.float32)),
        "wv": np.ascontiguousarray(np.asarray(L0_attn_Wv, dtype=np.float32)),
        "wo": np.ascontiguousarray(np.asarray(L0_attn_Wo, dtype=np.float32)),
        "wff": np.ascontiguousarray(np.asarray(L0_ffn_weight, dtype=np.float32)),
        "bias": np.ascontiguousarray(np.asarray(L0_ffn_bias, dtype=np.float32)),
    }
    if _NC_CACHE is None:
        _NC_CACHE = build_nc()
    in_maps = [dict(ins, x=xh[c * BPC:(c + 1) * BPC]) for c in range(8)]
    res = run_bass_kernel_spmd(_NC_CACHE, in_maps, core_ids=list(range(8)))
    return np.concatenate([r["out"] for r in res.results], axis=0)
